# version 02
# baseline (speedup 1.0000x reference)
import numpy as np
import concourse.bass as bass
import concourse.mybir as mybir
from concourse.bass_utils import run_bass_kernel_spmd

F32 = mybir.dt.float32
I32 = mybir.dt.int32
BF16 = mybir.dt.bfloat16
AF = mybir.ActivationFunctionType
ALU = mybir.AluOpType
AX = mybir.AxisListType

HID = 2048
TCORE = 4096
NKC = 16
TC = 512
NTC = 8
NHEAD_A = 16
NBANK = 4
MLP = 16384
PROJ = 10304
INTER = 4096
CONVD = 6144
NH = 64
HD = 64
NG = 8
SS = 128
CH = 256
SEQ = 2048
EPS_LN = 1e-5
EPS_A = 1e-6
EPS_RMS = 1e-6

ENGS = ("sync", "gpsimd", "vector", "scalar", "tensor")


def _parity_perm(n):
    return np.arange(n).reshape(-1, 128, 2).transpose(0, 2, 1).reshape(-1)


class Sem:
    def __init__(self, nc, name):
        self.h = nc.alloc_semaphore(name)
        self.n = 0

    def bump(self, k=1):
        self.n += k
        return self.n


class Ctx:
    def __init__(self, nc, block):
        self.nc = nc
        self.block = block
        self.ops = {e: [] for e in ENGS}
        self.pending = []
        self.bar = Sem(nc, "bar")
        self.nsem = 0

    def sem(self, name=None):
        self.nsem += 1
        return Sem(self.nc, name or f"s{self.nsem}")

    def sem_cached(self, key):
        if not hasattr(self, "_semcache"):
            self._semcache = {}
        if key not in self._semcache:
            self._semcache[key] = self.sem(key)
        return self._semcache[key]

    def emit(self, eng, fn):
        self.ops[eng].append(fn)

    def flush(self):
        for e in ENGS:
            fns = self.ops[e]
            if not fns:
                continue

            def body(engine, fns=tuple(fns)):
                for f in fns:
                    f(engine)
            getattr(self.block, e)(body)
            self.ops[e] = []

    def barrier(self):
        pend = list(self.pending)
        self.pending = []
        self.bar.bump(5)
        k = self.bar.n

        def f_sync(e, pend=pend, k=k):
            for (s, v) in pend:
                e.wait_ge(s.h, v)
            e.sem_inc(self.bar.h, 1)
            e.wait_ge(self.bar.h, k)
        self.emit("sync", f_sync)
        for eng in ("tensor", "vector", "scalar", "gpsimd"):
            def f_e(e, k=k):
                e.drain().then_inc(self.bar.h, 1)
                e.wait_ge(self.bar.h, k)
            self.emit(eng, f_e)
        self.flush()


class Bank:
    def __init__(self, ctx, i):
        self.ap = ctx.nc.alloc_psum_tensor(f"bank{i}", [128, 512], F32)
        self.done = ctx.sem(f"bkd{i}")
        self.uses = 0

    def f32(self, p=128, f=512):
        return self.ap[0:p, 0:f]

    def bf16(self, p=128, f=512):
        return self.ap[:].bitcast(BF16)[0:p, 0:f]


class BankPool:
    def __init__(self, ctx, n=8):
        self.banks = [Bank(ctx, i) for i in range(n)]
        self.free = list(self.banks)

    def acquire(self, ctx, prod_eng="tensor"):
        b = self.free.pop(0)
        need = b.uses
        b.uses += 1
        if need > 0:
            ctx.emit(prod_eng, lambda e, b=b, need=need: e.wait_ge(b.done.h, need))
        return b

    def release(self, b):
        self.free.append(b)


class KB:
    pass


def chunk_ap(d, c, tc=TC):
    return (d.rearrange("k (c t) -> k c t", t=tc)[:, c, :]
             .rearrange("(a p) t -> p a t", p=128))


def sb3(ap, tc=TC):
    return ap.rearrange("p (a t) -> p a t", t=tc)


def build(upto=8, dbg=False):
    nc = bass.Bass("TRN2", target_bir_lowering=False, debug=False)
    kb = KB()
    kb.nc = nc

    def din(name, shape, dt=F32):
        return nc.dram_tensor(name, shape, dt, kind="ExternalInput").ap()

    def dscr(name, shape, dt, out=False):
        kind = "ExternalOutput" if out else "Internal"
        return nc.dram_tensor(name, shape, dt, kind=kind).ap()

    kb.d_xt = din("xt", [HID, TCORE])
    kb.d_wqT = din("wqT", [HID, HID])
    kb.d_woT = din("woT", [HID, HID])
    kb.d_memT = din("memT", [128, NBANK * NHEAD_A * 128])
    kb.d_mnT = din("mnT", [128, NHEAD_A * NBANK])
    kb.d_bankw = din("bankw", [NHEAD_A, NBANK])
    kb.d_mnrows = din("mnrows", [NBANK, HID])
    kb.d_inw = din("inw", [HID, PROJ])
    kb.d_outw = din("outw", [INTER, HID])
    kb.d_convw = din("convw", [128, 48 * 4])
    kb.d_convb = din("convb", [128, 48])
    kb.d_dtb = din("dtb", [NH, 1])
    kb.d_alog = din("alog", [NH, 1])
    kb.d_dvec = din("dvec", [NH, 1])
    kb.d_normw = din("normw", [128, 32])
    kb.d_gpT = din("gpT", [1024, MLP], I32)
    kb.d_gsT = din("gsT", [64, MLP])
    kb.d_upT = din("upT", [1024, MLP], I32)
    kb.d_usT = din("usT", [64, MLP])
    kb.d_dpT = din("dpT", [MLP // 2, HID], I32)
    kb.d_dsT = din("dsT", [MLP // 32, HID])

    kb.d_x1 = dscr("x1", [HID, TCORE], F32, out=(dbg and upto == 1))
    kb.d_comb = dscr("comb", [HID, TCORE], BF16, out=(dbg and upto == 1))
    kb.d_x2 = dscr("x2", [HID, TCORE], F32, out=(dbg and upto == 2))
    kb.d_inter = dscr("inter", [MLP, TCORE], BF16, out=(dbg and upto == 15))
    kb.d_wgC = dscr("wgC", [64, HID, 256], BF16)
    kb.d_wuC = dscr("wuC", [64, HID, 256], BF16)
    kb.d_wdC = dscr("wdC", [4, MLP, TC], BF16)
    kb.d_projG = dscr("projG", [INTER, TCORE], BF16, out=(dbg and upto == 25))
    kb.d_projH = dscr("projH", [CONVD, TCORE], BF16)
    kb.d_projD = dscr("projD", [NH, TCORE], F32)
    kb.d_ynT = dscr("ynT", [INTER, TCORE], BF16, out=(dbg and upto == 26))
    kb.d_x3 = dscr("x3", [HID, TCORE], F32, out=(dbg and upto == 3))
    kb.d_out = nc.dram_tensor("oxt", [HID, TCORE], F32, kind="ExternalOutput").ap()
    kb.dbg = dbg
    if dbg:
        kb.d_dbgb = dscr("dbgb", [HID, TCORE], BF16, out=True)
        kb.d_dbgr = dscr("dbgr", [16, 2048], F32, out=True)
        kb.d_dbgs = dscr("dbgs", [16, 2048], F32, out=True)

    kb.ident_bf = nc.alloc_sbuf_tensor("ident_bf", [128, 128], BF16)
    kb.ident_f = nc.alloc_sbuf_tensor("ident_f", [128, 128], F32)
    kb.ones_f = nc.alloc_sbuf_tensor("ones_f", [128, 128], F32)
    kb.epat = nc.alloc_sbuf_tensor("epat", [8, 128], F32)
    kb.tri = nc.alloc_sbuf_tensor("tri", [128, 512], F32)
    kb.wact = nc.alloc_sbuf_tensor("wact", [NBANK, NHEAD_A], F32)
    kb.sconvw = nc.alloc_sbuf_tensor("sconvw", [128, 192], F32)
    kb.sconvb = nc.alloc_sbuf_tensor("sconvb", [128, 48], F32)
    kb.snormw = nc.alloc_sbuf_tensor("snormw", [128, 32], F32)
    kb.sAcol = nc.alloc_sbuf_tensor("sAcol", [NH, 1], F32)
    kb.sDtb = nc.alloc_sbuf_tensor("sDtb", [NH, 1], F32)
    kb.sDbc = nc.alloc_sbuf_tensor("sDbc", [128, NH], F32)
    kb.srow = nc.alloc_sbuf_tensor("srow", [16, 2048], F32)
    kb.sconst = nc.alloc_sbuf_tensor("sconst", [128, 4], F32)

    AR_BYTES = 184320
    kb.AR = nc.alloc_sbuf_tensor("AR", [128, AR_BYTES // 4], F32)

    def cf(off, n, p=128):
        assert off % 4 == 0 and off + 4 * n <= AR_BYTES, (off, n)
        return kb.AR[0:p, off // 4:off // 4 + n]

    def cb(off, n, p=128):
        assert off % 2 == 0 and off + 2 * n <= AR_BYTES, (off, n)
        return kb.AR[:].bitcast(BF16)[0:p, off // 2:off // 2 + n]

    def ci(off, n, p=128):
        assert off % 4 == 0 and off + 4 * n <= AR_BYTES, (off, n)
        return kb.AR[:].bitcast(I32)[0:p, off // 4:off // 4 + n]

    kb.cf, kb.cb, kb.ci = cf, cb, ci

    ctx = Ctx(nc, None)
    kb.pool = BankPool(ctx, 8)

    with nc.Block() as block:
        ctx.block = block
        _phase_prep(ctx, kb)
        ctx.barrier()
        _phase_A1(ctx, kb)
        ctx.barrier()
        _phase_A2(ctx, kb)
        ctx.barrier()
        if upto >= 2:
            _phase_Fa(ctx, kb, call=1)
            ctx.barrier()
            _phase_Fb(ctx, kb, call=1)
            ctx.barrier()
        if upto >= 3:
            _phase_B1(ctx, kb)
            ctx.barrier()
            _phase_B2(ctx, kb)
            ctx.barrier()
            _phase_B3(ctx, kb)
            ctx.barrier()
        if upto >= 4:
            _phase_Fa(ctx, kb, call=2)
            ctx.barrier()
            _phase_Fb(ctx, kb, call=2)
            ctx.barrier()
        pend = list(ctx.pending)

        def fin(e, pend=pend):
            for (s, v) in pend:
                e.wait_ge(s.h, v)
        ctx.emit("sync", fin)
        ctx.flush()
    return nc


def _phase_prep(ctx, kb):
    nc = ctx.nc
    cf = kb.cf
    pool = kb.pool
    d = ctx.sem("prepd")
    g = ctx.sem("prepg")
    v = ctx.sem("prepv")
    t = ctx.sem("prept")

    BW, MR, SM, DP = 0, 1024, 16384, 20480
    loads = [(cf(BW, 4, 16), kb.d_bankw), (cf(MR, 2048, 4), kb.d_mnrows),
             (kb.sconvw[:], kb.d_convw), (kb.sconvb[:], kb.d_convb),
             (kb.snormw[:], kb.d_normw), (kb.sDtb[:], kb.d_dtb),
             (kb.sAcol[:], kb.d_alog), (cf(DP, 1, NH), kb.d_dvec)]

    def f_sync(e):
        for (dst, src) in loads:
            e.dma_start(out=dst, in_=src).then_inc(d.h, 16)
    ctx.emit("sync", f_sync)
    nload = d.bump(16 * len(loads))

    def f_g(e):
        e.memset(kb.ones_f[:], 1.0)
        for ident in (kb.ident_bf, kb.ident_f):
            e.memset(ident[:], 1.0)
            e.drain()
            e.affine_select(out=ident[:], in_=ident[:], compare_op=ALU.is_ge,
                            fill=0.0, base=0, pattern=[[-1, 128]], channel_multiplier=1)
            e.drain()
            e.affine_select(out=ident[:], in_=ident[:], compare_op=ALU.is_ge,
                            fill=0.0, base=0, pattern=[[1, 128]], channel_multiplier=-1)
        e.memset(kb.tri[:], 1.0)
        e.drain()
        e.affine_select(out=kb.tri[:, 0:256], in_=kb.tri[:, 0:256], compare_op=ALU.is_ge,
                        fill=0.0, base=0, pattern=[[1, 256]], channel_multiplier=-1)
        e.affine_select(out=kb.tri[:, 256:512], in_=kb.tri[:, 256:512], compare_op=ALU.is_ge,
                        fill=0.0, base=-128, pattern=[[1, 256]], channel_multiplier=-1)
        e.memset(kb.sconst[:, 0:1], EPS_LN)
        e.memset(kb.sconst[:, 1:2], EPS_RMS)
        e.memset(kb.sconst[:, 2:3], 1.0)
        e.memset(kb.sconst[:, 3:4], 0.0)
        e.memset(kb.epat[:], 1.0)
        e.drain()
        e.affine_select(out=kb.epat[:], in_=kb.epat[:], compare_op=ALU.is_ge,
                        fill=0.0, base=0, pattern=[[1, 128]], channel_multiplier=-16)
        e.drain()
        e.affine_select(out=kb.epat[:], in_=kb.epat[:], compare_op=ALU.is_ge,
                        fill=0.0, base=15, pattern=[[-1, 128]], channel_multiplier=16)
        e.wait_ge(d.h, nload)
        e.memset(cf(DP + 4, 63, NH), 0.0).then_inc(g.h, 1)
    ctx.emit("gpsimd", f_g)
    ng = g.bump(1)

    def f_v(e):
        e.wait_ge(d.h, nload)
        e.reduce_max(kb.srow[0:16, 0:1], cf(BW, 4, 16), axis=AX.X).then_inc(v.h, 1)
        e.wait_ge(v.h, 1)
        e.tensor_scalar(out=cf(SM, 4, 16), in0=cf(BW, 4, 16),
                        scalar1=kb.srow[0:16, 0:1], scalar2=None,
                        op0=ALU.subtract).then_inc(v.h, 1)
    ctx.emit("vector", f_v)
    v.bump(1)
    nv1 = v.bump(1)

    def f_s(e):
        e.wait_ge(v.h, nv1)
        e.activation(out=cf(SM, 4, 16), in_=cf(SM, 4, 16), func=AF.Exp)
        e.activation(out=kb.sAcol[:], in_=kb.sAcol[:], func=AF.Exp).then_inc(v.h, 1)
    ctx.emit("scalar", f_s)
    nv2 = v.bump(1)

    def f_v2(e):
        e.wait_ge(v.h, nv2)
        e.reduce_sum(kb.srow[0:16, 0:1], cf(SM, 4, 16), axis=AX.X).then_inc(v.h, 1)
        e.wait_ge(v.h, nv2 + 1)
        e.reciprocal(out=kb.srow[0:16, 0:1], in_=kb.srow[0:16, 0:1]).then_inc(v.h, 1)
        e.wait_ge(v.h, nv2 + 2)
        e.tensor_scalar(out=cf(SM, 4, 16), in0=cf(SM, 4, 16),
                        scalar1=kb.srow[0:16, 0:1], scalar2=None, op0=ALU.mult)
        e.memset(cf(SM + 16, 12, 16), 0.0)
        e.reduce_sum(kb.srow[0:4, 4:5], cf(MR, 2048, 4), axis=AX.X).then_inc(v.h, 1)
        e.wait_ge(v.h, nv2 + 3)
        e.tensor_scalar(out=kb.srow[0:4, 4:5], in0=kb.srow[0:4, 4:5],
                        scalar1=EPS_A, scalar2=None, op0=ALU.is_ge)
        e.tensor_scalar(out=kb.sAcol[:], in0=kb.sAcol[:], scalar1=-1.0, scalar2=None,
                        op0=ALU.mult).then_inc(v.h, 1)
    ctx.emit("vector", f_v2)
    v.bump(3)
    nv3 = v.bump(1)

    b0 = pool.acquire(ctx)
    b1 = pool.acquire(ctx)

    def f_t(e):
        e.wait_ge(v.h, nv3)
        e.wait_ge(g.h, ng)
        e.transpose(out=b0.f32(16, 16), in_=cf(SM, 16, 16), identity=kb.ident_f[0:16, 0:16])
        e.transpose(out=b1.f32(NH, NH), in_=cf(DP, NH, NH),
                    identity=kb.ident_f[0:NH, 0:NH]).then_inc(t.h, 1)
    ctx.emit("tensor", f_t)
    nt1 = t.bump(1)

    def f_v3(e):
        e.wait_ge(t.h, nt1)
        e.tensor_scalar(out=kb.wact[:], in0=b0.f32(4, 16),
                        scalar1=kb.srow[0:4, 4:5], scalar2=None,
                        op0=ALU.mult).then_inc(b0.done.h, 1)
        e.tensor_copy(out=kb.srow[0:1, 8:8 + NH], in_=b1.f32(1, NH)).then_inc(b1.done.h, 1)
        e.drain().then_inc(v.h, 1)
    ctx.emit("vector", f_v3)
    b0.done.bump(1)
    b1.done.bump(1)
    pool.release(b0)
    pool.release(b1)
    nv4 = v.bump(1)

    b2 = pool.acquire(ctx)

    def f_t2(e):
        e.wait_ge(v.h, nv4)
        e.matmul(out=b2.f32(128, NH), lhsT=kb.ones_f[0:1, :],
                 rhs=kb.srow[0:1, 8:8 + NH], start=True, stop=True).then_inc(t.h, 1)
    ctx.emit("tensor", f_t2)
    nt2 = t.bump(1)

    def f_v4(e):
        e.wait_ge(t.h, nt2)
        e.tensor_copy(out=kb.sDbc[:], in_=b2.f32(128, NH)).then_inc(b2.done.h, 1)
    ctx.emit("vector", f_v4)
    b2.done.bump(1)
    pool.release(b2)
    ctx.flush()


def _ln_chunk(ctx, kb, x_tile, xln_tile, x2scr, sems):
    pool = kb.pool
    sq, st, rw, bc, ap_ = sems["sq"], sems["st"], sems["rw"], sems["bc"], sems["ap"]
    bS = pool.acquire(ctx)
    bQ = pool.acquire(ctx)
    base_st = st.n
    base_ap = ap_.n
    for kc in range(NKC):
        def f_a(e, kc=kc, base_st=base_st, base_ap=base_ap):
            if base_st + kc > 0:
                e.wait_ge(st.h, base_st + kc)
            if kc == 0 and base_ap > 0:
                e.wait_ge(ap_.h, base_ap)
            e.activation(out=x2scr, in_=x_tile(kc), func=AF.Square).then_inc(sq.h, 1)
        ctx.emit("scalar", f_a)
        nsq = sq.bump(1)

        def f_t(e, kc=kc, nsq=nsq, bS=bS, bQ=bQ):
            e.wait_ge(sq.h, nsq)
            e.matmul(out=bS.f32(1, TC), lhsT=kb.ones_f[:, 0:1], rhs=x_tile(kc),
                     start=(kc == 0), stop=(kc == NKC - 1))
            e.matmul(out=bQ.f32(1, TC), lhsT=kb.ones_f[:, 0:1], rhs=x2scr,
                     start=(kc == 0), stop=(kc == NKC - 1)).then_inc(st.h, 1)
        ctx.emit("tensor", f_t)
        st.bump(1)
    nst = st.n
    mean_r = kb.srow[0:1, 0:TC]
    rstd_r = kb.srow[0:1, TC:2 * TC]
    tmp_r = kb.srow[0:1, 2 * TC:3 * TC]

    def f_v(e, bS=bS, bQ=bQ, nst=nst):
        e.wait_ge(st.h, nst)
        e.tensor_scalar(out=mean_r, in0=bS.f32(1, TC), scalar1=1.0 / HID, scalar2=None,
                        op0=ALU.mult).then_inc(bS.done.h, 1)
        e.tensor_scalar(out=rstd_r, in0=bQ.f32(1, TC), scalar1=1.0 / HID, scalar2=None,
                        op0=ALU.mult).then_inc(bQ.done.h, 1)
        e.drain()
        e.tensor_tensor(out=tmp_r, in0=mean_r, in1=mean_r, op=ALU.mult)
        e.drain()
        e.tensor_tensor(out=rstd_r, in0=rstd_r, in1=tmp_r, op=ALU.subtract).then_inc(rw.h, 1)
    ctx.emit("vector", f_v)
    bS.done.bump(1)
    bQ.done.bump(1)
    kb.pool.release(bS)
    kb.pool.release(bQ)
    nrw = rw.bump(1)

    def f_s(e, nrw=nrw):
        e.wait_ge(rw.h, nrw)
        e.activation(out=rstd_r, in_=rstd_r, func=AF.Sqrt, bias=kb.sconst[0:1, 0:1]).then_inc(rw.h, 1)
    ctx.emit("scalar", f_s)
    nrw2 = rw.bump(1)

    def f_v2(e, nrw2=nrw2):
        e.wait_ge(rw.h, nrw2)
        e.reciprocal(out=rstd_r, in_=rstd_r).then_inc(rw.h, 1)
    ctx.emit("vector", f_v2)
    nrw3 = rw.bump(1)

    bM = pool.acquire(ctx)
    bR = pool.acquire(ctx)

    def f_t2(e, nrw3=nrw3, bM=bM, bR=bR):
        e.wait_ge(rw.h, nrw3)
        e.matmul(out=bM.f32(128, TC), lhsT=kb.ones_f[0:1, :], rhs=mean_r,
                 start=True, stop=True)
        e.matmul(out=bR.f32(128, TC), lhsT=kb.ones_f[0:1, :], rhs=rstd_r,
                 start=True, stop=True).then_inc(bc.h, 1)
    ctx.emit("tensor", f_t2)
    nbc = bc.bump(1)
    for kc in range(NKC):
        def f_v3(e, kc=kc, bM=bM, bR=bR, nbc=nbc):
            if kc == 0:
                e.wait_ge(bc.h, nbc)
            e.drain()
            e.tensor_tensor(out=x2scr, in0=x_tile(kc), in1=bM.f32(128, TC), op=ALU.subtract)
            e.drain()
            m = e.tensor_tensor(out=xln_tile(kc), in0=x2scr, in1=bR.f32(128, TC), op=ALU.mult)
            m.then_inc(ap_.h, 1)
            if kc == NKC - 1:
                e.drain().then_inc(bM.done.h, 1)
                e.drain().then_inc(bR.done.h, 1)
        ctx.emit("vector", f_v3)
        ap_.bump(1)
    bM.done.bump(1)
    bR.done.bump(1)
    kb.pool.release(bM)
    kb.pool.release(bR)
    return ap_.n


def _phase_A1(ctx, kb):
    cf, cb = kb.cf, kb.cb
    pool = kb.pool
    X, SCR = 0, 32768
    WQ, MEM, MN = 34816, 100352, 116736
    XLN, SIG, SS_, CMB = 116864, 133248, 149632, 151680
    T1, T2, F4, SEL4, F4B = 168064, 170112, 172160, 174208, 176256

    dW = ctx.sem(); dX = ctx.sem(); sq = ctx.sem(); st = ctx.sem()
    rw = ctx.sem(); bc = ctx.sem(); ap_ = ctx.sem(); qs = ctx.sem()
    sg = ctx.sem(); fr = ctx.sem(); cm = ctx.sem(); so = ctx.sem()

    gsel = ctx.sem()

    def f_g(e):
        for n in range(NBANK):
            reg = cf(SEL4 + n * 512, 128, 4)
            e.memset(reg, 1.0)
            e.drain()
            e.affine_select(out=reg, in_=reg, compare_op=ALU.is_ge, fill=0.0,
                            base=-n, pattern=[[0, 128]], channel_multiplier=1)
            e.drain()
            e.affine_select(out=reg, in_=reg, compare_op=ALU.is_ge, fill=0.0,
                            base=n, pattern=[[0, 128]], channel_multiplier=-1)
        e.drain().then_inc(gsel.h, 1)
        for kc in range(NKC):
            e.dma_start(out=cb(WQ + kc * 4096, 2048),
                        in_=kb.d_wqT[kc * 128:(kc + 1) * 128, :]).then_inc(dW.h, 16)
        e.dma_start(out=cb(MEM, 8192), in_=kb.d_memT).then_inc(dW.h, 16)
        e.dma_start(out=cb(MN, 64), in_=kb.d_mnT).then_inc(dW.h, 16)
    ctx.emit("gpsimd", f_g)
    nW = dW.bump(16 * (NKC + 2))

    for c in range(NTC):
        nso_prev = so.n

        def f_sy(e, c=c):
            if c > 0:
                e.wait_ge(ap_.h, c * NKC)
            e.dma_start(out=sb3(cf(X, 8192)), in_=chunk_ap(kb.d_xt, c)).then_inc(dX.h, 16)
        ctx.emit("sync", f_sy)
        nX = dX.bump(16)

        for eng in ("tensor", "scalar", "vector"):
            def f_w(e, nX=nX, c=c, eng=eng):
                e.wait_ge(dX.h, nX)
                if c == 0 and eng == "tensor":
                    e.wait_ge(dW.h, nW)
            ctx.emit(eng, f_w)

        x_tile = lambda kc: cf(X + kc * 2048, TC)
        xln_tile = lambda kc: cb(XLN + kc * 1024, TC)
        nln = _ln_chunk(ctx, kb, x_tile, xln_tile, cf(SCR, TC),
                        dict(sq=sq, st=st, rw=rw, bc=bc, ap=ap_))
        if kb.dbg and c == 0:
            dbs = ctx.sem()

            def f_dbg(e, nln=nln):
                e.wait_ge(ap_.h, nln)
                e.dma_start(out=chunk_ap(kb.d_dbgb, 0), in_=sb3(cb(XLN, 8192))).then_inc(dbs.h, 16)
                e.dma_start(out=kb.d_dbgr, in_=kb.srow[:]).then_inc(dbs.h, 16)
            ctx.emit("sync", f_dbg)
            ctx.pending.append((dbs, dbs.bump(32)))

        for j in range(NKC):
            bq = pool.acquire(ctx)

            def f_t(e, j=j, bq=bq, nln=nln):
                if j == 0:
                    e.wait_ge(ap_.h, nln)
                for kc in range(NKC):
                    m = e.matmul(out=bq.f32(128, TC), lhsT=cb(WQ + kc * 4096 + j * 256, 128),
                                 rhs=xln_tile(kc), start=(kc == 0), stop=(kc == NKC - 1))
                m.then_inc(qs.h, 1)
            ctx.emit("tensor", f_t)
            nq = qs.bump(1)

            def f_v(e, bq=bq, nq=nq):
                e.wait_ge(qs.h, nq)
                e.tensor_scalar(out=cf(T1, 512), in0=bq.f32(128, TC), scalar1=0.0,
                                scalar2=None, op0=ALU.min).then_inc(sg.h, 1)
            ctx.emit("vector", f_v)
            n1 = sg.bump(1)

            def f_s(e, n1=n1):
                e.wait_ge(sg.h, n1)
                e.activation(out=cf(T2, 512), in_=cf(T1, 512), func=AF.Exp).then_inc(sg.h, 1)
            ctx.emit("scalar", f_s)
            n2 = sg.bump(1)

            def f_v2(e, j=j, bq=bq, n2=n2):
                e.wait_ge(sg.h, n2)
                e.scalar_tensor_tensor(out=cb(SIG + j * 1024, TC), in0=bq.f32(128, TC),
                                       scalar=0.0, in1=cf(T2, 512), op0=ALU.max,
                                       op1=ALU.add).then_inc(bq.done.h, 1)
                e.drain().then_inc(sg.h, 1)
            ctx.emit("vector", f_v2)
            bq.done.bump(1)
            pool.release(bq)
            sg.bump(1)
        nsig = sg.n
        if kb.dbg and c == 0:
            dbs2 = ctx.sem()

            def f_dbg2(e, nsig=nsig):
                e.wait_ge(sg.h, nsig)
                e.dma_start(out=chunk_ap(kb.d_dbgb, 1), in_=sb3(cb(SIG, 8192))).then_inc(dbs2.h, 16)
                e.dma_start(out=kb.d_dbgs[0:4, 0:512], in_=cf(SEL4, 512, 4)).then_inc(dbs2.h, 16)
                e.dma_start(out=kb.d_dbgs[4:8, 0:16], in_=kb.wact[:]).then_inc(dbs2.h, 16)
            ctx.emit("sync", f_dbg2)
            ctx.pending.append((dbs2, dbs2.bump(48)))

        for h in range(NHEAD_A):
            F4h = F4 if h % 2 == 0 else F4B
            sig_h = cb(SIG + h * 1024, TC)
            bn = pool.acquire(ctx)

            def f_t2(e, h=h, bn=bn, nsig=nsig, sig_h=sig_h):
                if h == 0:
                    e.wait_ge(sg.h, nsig)
                e.matmul(out=bn.f32(4, TC), lhsT=cb(MN + h * 8, 4), rhs=sig_h,
                         start=True, stop=True).then_inc(qs.h, 1)
            ctx.emit("tensor", f_t2)
            nn = qs.bump(1)

            def f_v3(e, h=h, bn=bn, nn=nn, nf0=fr.n, F4h=F4h):
                e.wait_ge(qs.h, nn)
                e.tensor_scalar(out=cf(F4h, 512, 4), in0=bn.f32(4, TC), scalar1=EPS_A,
                                scalar2=None, op0=ALU.max).then_inc(bn.done.h, 1)
                e.drain()
                e.reciprocal(out=cf(F4h, 512, 4), in_=cf(F4h, 512, 4)).then_inc(fr.h, 1)
                e.wait_ge(fr.h, nf0 + 1)
                e.tensor_scalar(out=cf(F4h, 512, 4), in0=cf(F4h, 512, 4),
                                scalar1=kb.wact[:, h:h + 1], scalar2=None,
                                op0=ALU.mult).then_inc(fr.h, 1)
            ctx.emit("vector", f_v3)
            bn.done.bump(1)
            pool.release(bn)
            fr.bump(1)
            nf = fr.bump(1)
            if kb.dbg and c == 0 and h == 0:
                dbs3 = ctx.sem()

                def f_dbg3(e, nf=nf):
                    e.wait_ge(fr.h, nf)
                    e.dma_start(out=kb.d_dbgs[8:12, 0:512], in_=cf(F4, 512, 4)).then_inc(dbs3.h, 16)
                ctx.emit("sync", f_dbg3)
                ctx.pending.append((dbs3, dbs3.bump(16)))
                ctx.barrier()

            bC = pool.acquire(ctx)
            for n in range(NBANK):
                bb = pool.acquire(ctx)

                def f_t3(e, n=n, bb=bb, nf=nf, F4h=F4h):
                    if n == 0:
                        e.wait_ge(fr.h, nf)
                        e.wait_ge(gsel.h, 1)
                    e.matmul(out=bb.f32(128, TC), lhsT=cf(SEL4 + n * 512, 128, 4),
                             rhs=cf(F4h, 512, 4), start=True,
                             stop=True).then_inc(qs.h, 1)
                ctx.emit("tensor", f_t3)
                nb = qs.bump(1)

                def f_v4(e, n=n, h=h, bb=bb, nb=nb, sig_h=sig_h):
                    e.wait_ge(qs.h, nb)
                    e.tensor_tensor(out=cb(SS_ + (n % 2) * 1024, TC), in0=sig_h,
                                    in1=bb.f32(128, TC), op=ALU.mult).then_inc(sg.h, 1)
                    e.drain().then_inc(bb.done.h, 1)
                ctx.emit("vector", f_v4)
                bb.done.bump(1)
                pool.release(bb)
                nss = sg.bump(1)

                def f_t4(e, n=n, h=h, bC=bC, nss=nss):
                    e.wait_ge(sg.h, nss)
                    m = e.matmul(out=bC.f32(128, TC),
                                 lhsT=cb(MEM + (n * 16 + h) * 256, 128),
                                 rhs=cb(SS_ + (n % 2) * 1024, TC),
                                 start=(n == 0), stop=(n == NBANK - 1))
                    m.then_inc(qs.h, 1)
                ctx.emit("tensor", f_t4)
                qs.bump(1)
            nq4 = qs.n

            def f_v5(e, h=h, c=c, bC=bC, nq4=nq4, nso_prev=nso_prev):
                e.wait_ge(qs.h, nq4)
                if h == 0 and nso_prev > 0:
                    e.wait_ge(so.h, nso_prev)
                e.tensor_copy(out=cb(CMB + h * 1024, TC),
                              in_=bC.f32(128, TC)).then_inc(bC.done.h, 1)
                e.drain().then_inc(cm.h, 1)
            ctx.emit("vector", f_v5)
            bC.done.bump(1)
            pool.release(bC)
            cm.bump(1)
        ncall = cm.n

        def f_sy2(e, c=c, ncall=ncall):
            e.wait_ge(cm.h, ncall)
            e.dma_start(out=chunk_ap(kb.d_comb, c), in_=sb3(cb(CMB, 8192))).then_inc(so.h, 16)
        ctx.emit("sync", f_sy2)
        so.bump(16)

    ctx.pending.append((so, so.n))
    ctx.flush()


def _phase_A2(ctx, kb):
    cf, cb = kb.cf, kb.cb
    pool = kb.pool
    X, WO, CMB, X1 = 0, 34816, 100352, 118784

    dW = ctx.sem(); dX = ctx.sem(); qs = ctx.sem(); ev = ctx.sem(); so = ctx.sem()

    def f_g(e):
        for kc in range(NKC):
            e.dma_start(out=cb(WO + kc * 4096, 2048),
                        in_=kb.d_woT[kc * 128:(kc + 1) * 128, :]).then_inc(dW.h, 16)
    ctx.emit("gpsimd", f_g)
    nW = dW.bump(16 * NKC)

    for c in range(NTC):
        nso_prev = so.n

        def f_sy(e, c=c):
            if c > 0:
                e.wait_ge(ev.h, c * NKC)
            e.dma_start(out=sb3(cf(X, 8192)), in_=chunk_ap(kb.d_xt, c)).then_inc(dX.h, 16)
            e.dma_start(out=sb3(cb(CMB, 8192)), in_=chunk_ap(kb.d_comb, c)).then_inc(dX.h, 16)
        ctx.emit("sync", f_sy)
        nX = dX.bump(32)

        for j in range(NKC):
            bq = pool.acquire(ctx)

            def f_t(e, j=j, c=c, bq=bq, nX=nX):
                if j == 0:
                    e.wait_ge(dX.h, nX)
                    if c == 0:
                        e.wait_ge(dW.h, nW)
                for kc in range(NKC):
                    m = e.matmul(out=bq.f32(128, TC), lhsT=cb(WO + kc * 4096 + j * 256, 128),
                                 rhs=cb(CMB + kc * 1024, TC), start=(kc == 0),
                                 stop=(kc == NKC - 1))
                m.then_inc(qs.h, 1)
            ctx.emit("tensor", f_t)
            nq = qs.bump(1)

            def f_v(e, j=j, bq=bq, nq=nq, nso_prev=nso_prev):
                e.wait_ge(qs.h, nq)
                if j == 0 and nso_prev > 0:
                    e.wait_ge(so.h, nso_prev)
                e.tensor_tensor(out=cf(X1 + j * 2048, TC), in0=cf(X + j * 2048, TC),
                                in1=bq.f32(128, TC), op=ALU.add).then_inc(bq.done.h, 1)
                e.drain().then_inc(ev.h, 1)
            ctx.emit("vector", f_v)
            bq.done.bump(1)
            pool.release(bq)
            ev.bump(1)
        nev = ev.n

        def f_sy2(e, c=c, nev=nev):
            e.wait_ge(ev.h, nev)
            e.dma_start(out=chunk_ap(kb.d_x1, c), in_=sb3(cf(X1, 8192))).then_inc(so.h, 16)
        ctx.emit("sync", f_sy2)
        so.bump(16)

    ctx.pending.append((so, so.n))
    ctx.flush()


WN = 256
NWIN = MLP // WN


def _dequant_tile(ctx, kb, eng, pk_i32, scale_psum, out_lo, out_hi, wid, dqo):
    ci = kb.ci
    DQ1, DQ2, DQB = dqo

    def f(e, pk=pk_i32, sp=scale_psum, out_lo=out_lo, out_hi=out_hi):
        for (shift, out) in ((0, out_lo), (4, out_hi)):
            if shift == 0:
                e.tensor_scalar(out=ci(DQ1, wid), in0=pk, scalar1=7, scalar2=None,
                                op0=ALU.bitwise_and)
                e.tensor_scalar(out=ci(DQ2, wid), in0=pk, scalar1=8, scalar2=28,
                                op0=ALU.bitwise_and, op1=ALU.logical_shift_left)
            else:
                e.tensor_scalar(out=ci(DQ1, wid), in0=pk, scalar1=4, scalar2=7,
                                op0=ALU.logical_shift_right, op1=ALU.bitwise_and)
                e.tensor_scalar(out=ci(DQ2, wid), in0=pk, scalar1=7, scalar2=31,
                                op0=ALU.logical_shift_right, op1=ALU.logical_shift_left)
            e.drain()
            e.tensor_scalar(out=ci(DQB, wid), in0=ci(DQ1, wid), scalar1=252,
                            scalar2=None, op0=ALU.mult)
            e.drain()
            e.scalar_tensor_tensor(out=ci(DQB, wid), in0=ci(DQ1, wid), scalar=252,
                                   in1=ci(DQB, wid), op0=ALU.add, op1=ALU.min)
            e.drain()
            e.scalar_tensor_tensor(out=ci(DQB, wid), in0=ci(DQB, wid),
                                   scalar=4194304, in1=ci(DQ2, wid),
                                   op0=ALU.mult, op1=ALU.add)
            e.drain()
            e.tensor_tensor(out=out, in0=ci(DQB, wid).bitcast(F32), in1=sp,
                            op=ALU.mult)
            e.drain()
    ctx.emit(eng, f)


def _phase_Fa(ctx, kb, call):
    cf, cb, ci = kb.cf, kb.cb, kb.ci
    pool = kb.pool
    xsrc = kb.d_x1 if call == 1 else kb.d_x3

    XLN = 0
    WG = 131072
    WU = 147456
    PK = 163840
    DQ1, DQ2, DQB = 165888, 166912, 167936
    INTER = 168960
    GS = 177152
    XB = 131072
    SCRB = 167936

    S = ctx.sem_cached
    dX = S("FadX"); sq = S("Fasq"); st = S("Fast"); rw = S("Farw")
    bc = S("Fabc"); ap_ = S("Faap"); dqv = S("Fadqv")
    mmv = S("Fammv"); mmw = S("Fammw"); iv = S("Faiv")
    so = S("Faso"); dsc = S("Fadsc")
    dqs = [S("Fadqs0"), S("Fadqs1")]
    css = [S("Facss0"), S("Facss1")]
    dqw = [S("Fadqw0"), S("Fadqw1")]

    base_ap = ap_.n
    for c in range(NTC):
        def f_sy(e, c=c, base_ap=base_ap):
            if c > 0:
                e.wait_ge(ap_.h, base_ap + c * NKC)
            e.dma_start(out=sb3(cf(XB, 8192)), in_=chunk_ap(xsrc, c)).then_inc(dX.h, 16)
        ctx.emit("sync", f_sy)
        nX = dX.bump(16)
        for eng in ("tensor", "scalar", "vector"):
            ctx.emit(eng, lambda e, nX=nX: e.wait_ge(dX.h, nX))
        x_tile = lambda kc, c=c: cf(XB + kc * 2048, TC)
        xln_tile = lambda kc, c=c: cb(XLN + kc * 8192 + c * 1024, TC)
        _ln_chunk(ctx, kb, x_tile, xln_tile, cf(SCRB, TC),
                  dict(sq=sq, st=st, rw=rw, bc=bc, ap=ap_))
    ctx.pending.append((ap_, ap_.n))
    ctx.barrier()

    wg_view = cb(WG, 2 * 16 * WN).rearrange("p (b k n) -> p b k n", b=2, n=WN)
    wu_view = cb(WU, 2 * 16 * WN).rearrange("p (b k n) -> p b k n", b=2, n=WN)
    last_dsc = [dsc.n, dsc.n]
    base_mmw = mmw.n
    base_mmv = mmv.n

    for w in range(NWIN):
        buf = w % 2
        if call == 1:
            for mi in range(2):
                dpk = kb.d_gpT if mi == 0 else kb.d_upT
                dsct = kb.d_gsT if mi == 0 else kb.d_usT
                wv = wg_view if mi == 0 else wu_view
                for cc in range(8):
                    slot = (w * 16 + mi * 8 + cc) % 2

                    dqsem = dqs[slot]

                    def f_ld(e, w=w, cc=cc, dpk=dpk, dsct=dsct, mi=mi, slot=slot,
                             scgate=last_dsc[mi], pkgate=dqv.n - 1, dqsem=dqsem):
                        if pkgate > 0:
                            e.wait_ge(dqv.h, pkgate)
                        if scgate > 0:
                            e.wait_ge(dsc.h, scgate)
                        e.dma_start(out=ci(PK + slot * 1024, WN),
                                    in_=dpk[cc * 128:(cc + 1) * 128,
                                            w * WN:(w + 1) * WN]).then_inc(dqsem.h, 16)
                        e.dma_start(out=kb.srow[0:8, mi * 512:mi * 512 + WN],
                                    in_=dsct[8 * cc:8 * cc + 8,
                                             w * WN:(w + 1) * WN]).then_inc(dqsem.h, 16)
                    ctx.emit("sync", f_ld)
                    ndq = dqsem.bump(32)

                    psc = pool.acquire(ctx)

                    def f_sc(e, psc=psc, ndq=ndq, mi=mi, dqsem=dqsem):
                        e.wait_ge(dqsem.h, ndq)
                        e.matmul(out=psc.f32(128, WN), lhsT=kb.epat[:],
                                 rhs=kb.srow[0:8, mi * 512:mi * 512 + WN],
                                 start=True, stop=True).then_inc(dsc.h, 1)
                    ctx.emit("tensor", f_sc)
                    last_dsc[mi] = dsc.bump(1)

                    def f_gate(e, ndsc=last_dsc[mi], w=w, csgate=css[buf].n,
                               bw=base_mmw, bv=base_mmv):
                        e.wait_ge(dsc.h, ndsc)
                        if w >= 2:
                            e.wait_ge(mmw.h, bw + (w - 1) * 16)
                            e.wait_ge(mmv.h, bv + (w - 1) * 16)
                            e.wait_ge(css[w % 2].h, csgate)
                    ctx.emit("vector", f_gate)
                    _dequant_tile(ctx, kb, "vector", ci(PK + slot * 1024, WN),
                                  psc.f32(128, WN), wv[:, buf, 2 * cc, :],
                                  wv[:, buf, 2 * cc + 1, :], WN, (DQ1, DQ2, DQB))

                    def f_dqd(e, psc=psc):
                        e.drain().then_inc(psc.done.h, 1)
                        e.drain().then_inc(dqv.h, 1)
                    ctx.emit("vector", f_dqd)
                    psc.done.bump(1)
                    pool.release(psc)
                    dqv.bump(1)

            def f_cs(e, w=w, buf=buf, nv=dqv.n):
                e.wait_ge(dqv.h, nv)
                e.dma_start(out=kb.d_wgC[w].rearrange("(a p) n -> p a n", p=128),
                            in_=cb(WG + buf * 8192, 16 * WN)
                            .rearrange("p (a n) -> p a n", n=WN)).then_inc(css[buf].h, 16)
                e.dma_start(out=kb.d_wuC[w].rearrange("(a p) n -> p a n", p=128),
                            in_=cb(WU + buf * 8192, 16 * WN)
                            .rearrange("p (a n) -> p a n", n=WN)).then_inc(css[buf].h, 16)
            ctx.emit("scalar", f_cs)
            css[buf].bump(32)
        else:
            def f_ldc(e, w=w, buf=buf, bw=base_mmw, bv=base_mmv):
                if w >= 2:
                    e.wait_ge(mmw.h, bw + (w - 1) * 16)
                    e.wait_ge(mmv.h, bv + (w - 1) * 16)
                e.dma_start(out=cb(WG + buf * 8192, 16 * WN)
                            .rearrange("p (a n) -> p a n", n=WN),
                            in_=kb.d_wgC[w].rearrange("(a p) n -> p a n", p=128)
                            ).then_inc(dqw[buf].h, 16)
                e.dma_start(out=cb(WU + buf * 8192, 16 * WN)
                            .rearrange("p (a n) -> p a n", n=WN),
                            in_=kb.d_wuC[w].rearrange("(a p) n -> p a n", p=128)
                            ).then_inc(dqw[buf].h, 16)
            ctx.emit("sync", f_ldc)
            dqw[buf].bump(32)

        ndq_need = dqv.n if call == 1 else dqw[buf].n
        rdy = dqv if call == 1 else dqw[buf]
        for nj in range(2):
            for tc in range(NTC):
                gslot = (w * 16 + nj * 8 + tc) % 2
                pg = pool.acquire(ctx)
                pu = pool.acquire(ctx)

                def f_mm(e, buf=buf, nj=nj, tc=tc, pg=pg, pu=pu,
                         ndq_need=ndq_need, rdy=rdy, first=(nj == 0 and tc == 0)):
                    if first:
                        e.wait_ge(rdy.h, ndq_need)
                    for kbk in range(NKC):
                        m = e.matmul(out=pg.f32(128, TC),
                                     lhsT=cb(WG + buf * 8192 + kbk * 512 + nj * 256, 128),
                                     rhs=cb(XLN + kbk * 8192 + tc * 1024, TC),
                                     start=(kbk == 0), stop=(kbk == NKC - 1))
                    m.then_inc(mmv.h, 1)
                    for kbk in range(NKC):
                        m = e.matmul(out=pu.f32(128, TC),
                                     lhsT=cb(WU + buf * 8192 + kbk * 512 + nj * 256, 128),
                                     rhs=cb(XLN + kbk * 8192 + tc * 1024, TC),
                                     start=(kbk == 0), stop=(kbk == NKC - 1))
                    m.then_inc(mmw.h, 1)
                ctx.emit("tensor", f_mm)
                nv = mmv.bump(1)
                nw = mmw.bump(1)

                def f_gelu(e, pg=pg, nv=nv, gslot=gslot, ivgate=iv.n - 1):
                    e.wait_ge(mmv.h, nv)
                    if ivgate > 0:
                        e.wait_ge(iv.h, ivgate)
                    e.activation(out=cf(GS + gslot * 2048, 512), in_=pg.f32(128, TC),
                                 func=AF.Gelu_apprx_tanh).then_inc(pg.done.h, 1)
                ctx.emit("scalar", f_gelu)
                npgd = pg.done.bump(1)
                pool.release(pg)

                def f_int(e, pu=pu, nw=nw, npgd=npgd, pg=pg, tc=tc, nso=so.n,
                          gslot=gslot):
                    e.wait_ge(mmw.h, nw)
                    e.wait_ge(pg.done.h, npgd)
                    if nso > 0:
                        e.wait_ge(so.h, nso)
                    e.tensor_tensor(out=cb(INTER + tc * 1024, TC),
                                    in0=cf(GS + gslot * 2048, 512),
                                    in1=pu.f32(128, TC), op=ALU.mult).then_inc(pu.done.h, 1)
                    e.drain().then_inc(iv.h, 1)
                ctx.emit("vector", f_int)
                pu.done.bump(1)
                pool.release(pu)
                iv.bump(1)

            def f_si(e, w=w, nj=nj, niv=iv.n):
                e.wait_ge(iv.h, niv)
                rb = w * 2 + nj
                e.dma_start(out=kb.d_inter[rb * 128:(rb + 1) * 128, :],
                            in_=cb(INTER, 4096)).then_inc(so.h, 16)
            ctx.emit("sync", f_si)
            so.bump(16)

    ctx.pending.append((so, so.n))
    if call == 1:
        ctx.pending.append((css[0], css[0].n))
        ctx.pending.append((css[1], css[1].n))
    ctx.flush()


def _phase_Fb(ctx, kb, call):
    cf, cb, ci = kb.cf, kb.cb, kb.ci
    pool = kb.pool
    x_in = kb.d_x1 if call == 1 else kb.d_x3
    x_out = kb.d_x2 if call == 1 else kb.d_out

    INTERS = 0
    WD = 131072
    PKD = 147456
    DQ1, DQ2, DQB = 151552, 153600, 155648
    X1B = 157696
    X2B = 161792

    S = ctx.sem_cached
    dI = S("FbdI"); dsc = S("Fbdsc"); dqv = S("Fbdqv")
    mmu = S("Fbmmu"); ev = S("Fbev")
    dqs = [S("Fbdqs0"), S("Fbdqs1")]
    css = [S("Fbcss0"), S("Fbcss1")]
    dws = [S("Fbdws0"), S("Fbdws1")]
    dx1s = [S("Fbdx1s0"), S("Fbdx1s1")]
    sos = [S("Fbsos0"), S("Fbsos1")]

    for tc in range(NTC):
        def f_li(e, tc=tc, nmu=mmu.n):
            if nmu > 0:
                e.wait_ge(mmu.h, nmu)
            for part in range(8):
                e.dma_start(out=sb3(cb(INTERS + part * 16 * 1024, 16 * TC)),
                            in_=kb.d_inter.rearrange("k (c t) -> k c t", t=TC)
                            [part * 2048:(part + 1) * 2048, tc, :]
                            .rearrange("(a p) t -> p a t", p=128)).then_inc(dI.h, 16)
        ctx.emit("sync", f_li)
        ndI = dI.bump(16 * 8)

        for hw in range(4):
            acc = [pool.acquire(ctx) for _ in range(4)]

            def f_tw(e, ndI=ndI, first=(hw == 0)):
                if first:
                    e.wait_ge(dI.h, ndI)
            ctx.emit("tensor", f_tw)

            if call == 1 and tc == 0:
                base_mmu = mmu.n
                base_dqv = dqv.n
                base_dsc = dsc.n
                for npc in range(64):
                    slot = npc % 2

                    dqsem = dqs[slot]

                    def f_lp(e, npc=npc, hw=hw, slot=slot,
                             pkgate=(base_dqv + npc - 1), dqsem=dqsem,
                             scgate=(base_dsc + npc)):
                        if pkgate > 0:
                            e.wait_ge(dqv.h, pkgate)
                        if scgate > 0:
                            e.wait_ge(dsc.h, scgate)
                        e.dma_start(out=ci(PKD + slot * 2048, 512),
                                    in_=kb.d_dpT[npc * 128:(npc + 1) * 128,
                                                 hw * 512:(hw + 1) * 512]).then_inc(dqsem.h, 16)
                        e.dma_start(out=kb.srow[0:8, 0:512],
                                    in_=kb.d_dsT[8 * npc:8 * npc + 8,
                                                 hw * 512:(hw + 1) * 512]).then_inc(dqsem.h, 16)
                    ctx.emit("sync", f_lp)
                    ndq = dqsem.bump(32)
                    psc = pool.acquire(ctx)

                    def f_sc(e, psc=psc, ndq=ndq, dqsem=dqsem):
                        e.wait_ge(dqsem.h, ndq)
                        e.matmul(out=psc.f32(128, TC), lhsT=kb.epat[:],
                                 rhs=kb.srow[0:8, 0:512], start=True,
                                 stop=True).then_inc(dsc.h, 1)
                    ctx.emit("tensor", f_sc)
                    ndsc = dsc.bump(1)

                    def f_gate(e, ndsc=ndsc, npc=npc,
                               mgate=(base_mmu + npc - 1),
                               cgate=css[slot].n):
                        e.wait_ge(dsc.h, ndsc)
                        if mgate > 0:
                            e.wait_ge(mmu.h, mgate)
                        if cgate > 0:
                            e.wait_ge(css[npc % 2].h, cgate)
                    ctx.emit("vector", f_gate)
                    _dequant_tile(ctx, kb, "vector", ci(PKD + slot * 2048, 512),
                                  psc.f32(128, TC),
                                  cb(WD + (slot * 2 + 0) * 1024, 512),
                                  cb(WD + (slot * 2 + 1) * 1024, 512), 512,
                                  (DQ1, DQ2, DQB))

                    def f_dqd(e, psc=psc):
                        e.drain().then_inc(psc.done.h, 1)
                        e.drain().then_inc(dqv.h, 1)
                    ctx.emit("vector", f_dqd)
                    psc.done.bump(1)
                    pool.release(psc)
                    ndqv = dqv.bump(1)

                    def f_mm(e, npc=npc, slot=slot, acc=tuple(acc), ndqv=ndqv):
                        e.wait_ge(dqv.h, ndqv)
                        for nib in range(2):
                            for hj in range(4):
                                m = e.matmul(out=acc[hj].f32(128, TC),
                                             lhsT=cb(WD + (slot * 2 + nib) * 1024 + hj * 256, 128),
                                             rhs=cb(INTERS + (2 * npc + nib) * 1024, TC),
                                             start=(npc == 0 and nib == 0),
                                             stop=(npc == 63 and nib == 1))
                        m.then_inc(mmu.h, 1)
                    ctx.emit("tensor", f_mm)
                    mmu.bump(1)

                    def f_cst(e, npc=npc, slot=slot, hw=hw, ndqv=ndqv):
                        e.wait_ge(dqv.h, ndqv)
                        e.dma_start(out=kb.d_wdC[hw][2 * npc * 128:(2 * npc + 2) * 128, :]
                                    .rearrange("(a p) n -> p a n", p=128),
                                    in_=cb(WD + slot * 2048, 2 * 512)
                                    .rearrange("p (a n) -> p a n", n=512)).then_inc(css[slot].h, 16)
                    ctx.emit("scalar", f_cst)
                    css[slot].bump(16)
            else:
                base_mmu = mmu.n
                for grp in range(16):
                    slot = grp % 2

                    def f_lw(e, grp=grp, hw=hw, slot=slot,
                             gate=(base_mmu + grp - 1),
                             csgate0=css[0].n, csgate1=css[1].n):
                        if gate > 0:
                            e.wait_ge(mmu.h, gate)
                        if csgate0 > 0:
                            e.wait_ge(css[0].h, csgate0)
                        if csgate1 > 0:
                            e.wait_ge(css[1].h, csgate1)
                        e.dma_start(out=cb(WD + slot * 8192, 8 * 512)
                                    .rearrange("p (a n) -> p a n", n=512),
                                    in_=kb.d_wdC[hw][grp * 1024:(grp + 1) * 1024, :]
                                    .rearrange("(a p) n -> p a n", p=128)).then_inc(dws[slot].h, 16)
                    ctx.emit("sync", f_lw)
                    ndw = dws[slot].bump(16)

                    def f_mm(e, grp=grp, slot=slot, acc=tuple(acc), ndw=ndw):
                        e.wait_ge(dws[slot].h, ndw)
                        for b8 in range(8):
                            ncb = grp * 8 + b8
                            for hj in range(4):
                                m = e.matmul(out=acc[hj].f32(128, TC),
                                             lhsT=cb(WD + slot * 8192 + b8 * 1024 + hj * 256, 128),
                                             rhs=cb(INTERS + ncb * 1024, TC),
                                             start=(ncb == 0), stop=(ncb == 127))
                        m.then_inc(mmu.h, 1)
                    ctx.emit("tensor", f_mm)
                    mmu.bump(1)

            for hj in range(4):
                slot = hj % 2
                hr = hw * 4 + hj

                def f_lx(e, hr=hr, tc=tc, slot=slot, nev=ev.n - 1):
                    if nev > 0:
                        e.wait_ge(ev.h, nev)
                    e.dma_start(out=cf(X1B + slot * 2048, 512),
                                in_=x_in.rearrange("k (c t) -> k c t", t=TC)
                                [hr * 128:(hr + 1) * 128, tc, :]).then_inc(dx1s[slot].h, 16)
                ctx.emit("sync", f_lx)
                ndx = dx1s[slot].bump(16)
                nmu_need = mmu.n

                def f_ev(e, hj=hj, slot=slot, acc=tuple(acc), ndx=ndx,
                         nmu_need=nmu_need, sgate=sos[slot].n):
                    e.wait_ge(mmu.h, nmu_need)
                    e.wait_ge(dx1s[slot].h, ndx)
                    if sgate > 0:
                        e.wait_ge(sos[slot].h, sgate)
                    e.tensor_tensor(out=cf(X2B + slot * 2048, 512),
                                    in0=cf(X1B + slot * 2048, 512),
                                    in1=acc[hj].f32(128, TC),
                                    op=ALU.add).then_inc(acc[hj].done.h, 1)
                    e.drain().then_inc(ev.h, 1)
                ctx.emit("vector", f_ev)
                acc[hj].done.bump(1)
                pool.release(acc[hj])
                ev.bump(1)

                def f_sx(e, hr=hr, tc=tc, slot=slot, nev=ev.n):
                    e.wait_ge(ev.h, nev)
                    e.dma_start(out=x_out.rearrange("k (c t) -> k c t", t=TC)
                                [hr * 128:(hr + 1) * 128, tc, :],
                                in_=cf(X2B + slot * 2048, 512)).then_inc(sos[slot].h, 16)
                ctx.emit("sync", f_sx)
                sos[slot].bump(16)

    ctx.pending.append((sos[0], sos[0].n))
    ctx.pending.append((sos[1], sos[1].n))
    if call == 1:
        ctx.pending.append((css[0], css[0].n))
        ctx.pending.append((css[1], css[1].n))
    ctx.flush()


def _phase_B1(ctx, kb):
    cf, cb = kb.cf, kb.cb
    pool = kb.pool

    XLN = 0
    XB = 131072
    SCRB = 163840
    WB = 131072
    PS = 139264
    PSD = 147456

    dX = ctx.sem(); sq = ctx.sem(); st = ctx.sem(); rw = ctx.sem()
    bc = ctx.sem(); ap_ = ctx.sem(); mm = ctx.sem(); evs = ctx.sem()
    so = ctx.sem()
    dws = [ctx.sem(), ctx.sem()]

    for c in range(NTC):
        def f_sy(e, c=c):
            if c > 0:
                e.wait_ge(ap_.h, c * NKC)
            e.dma_start(out=sb3(cf(XB, 8192)), in_=chunk_ap(kb.d_x2, c)).then_inc(dX.h, 16)
        ctx.emit("sync", f_sy)
        nX = dX.bump(16)
        for eng in ("tensor", "scalar", "vector"):
            ctx.emit(eng, lambda e, nX=nX: e.wait_ge(dX.h, nX))
        x_tile = lambda kc, c=c: cf(XB + kc * 2048, TC)
        xln_tile = lambda kc, c=c: cb(XLN + kc * 8192 + c * 1024, TC)
        _ln_chunk(ctx, kb, x_tile, xln_tile, cf(SCRB, TC),
                  dict(sq=sq, st=st, rw=rw, bc=bc, ap=ap_))
    ctx.pending.append((ap_, ap_.n))
    ctx.barrier()

    NPJ = 81
    for j in range(NPJ):
        wid = 128 if j < 80 else 64
        slot = j % 2

        def f_lw(e, j=j, slot=slot, wid=wid, gate=mm.n - 8):
            if gate > 0:
                e.wait_ge(mm.h, gate)
            e.dma_start(out=cb(WB + slot * 4096, 16 * 128)
                        .rearrange("p (a n) -> p a n", n=128)[:, :, 0:wid],
                        in_=kb.d_inw.rearrange("(a p) n -> p a n", p=128)
                        [:, :, j * 128:j * 128 + wid]).then_inc(dws[slot].h, 16)
        ctx.emit("gpsimd", f_lw)
        ndw = dws[slot].bump(16)

        for tc in range(NTC):
            pq = pool.acquire(ctx)

            def f_mm(e, j=j, tc=tc, slot=slot, pq=pq, ndw=ndw, wid=wid):
                if tc == 0:
                    e.wait_ge(dws[slot].h, ndw)
                for kbk in range(NKC):
                    m = e.matmul(out=pq.f32(wid, TC),
                                 lhsT=cb(WB + slot * 4096 + kbk * 256, 128)[:, 0:wid],
                                 rhs=cb(XLN + kbk * 8192 + tc * 1024, TC),
                                 start=(kbk == 0), stop=(kbk == NKC - 1))
                m.then_inc(mm.h, 1)
            ctx.emit("tensor", f_mm)
            nmm = mm.bump(1)

            def f_ev(e, j=j, tc=tc, pq=pq, nmm=nmm, wid=wid, sgate=so.n):
                e.wait_ge(mm.h, nmm)
                if tc == 0 and sgate > 0:
                    e.wait_ge(so.h, sgate)
                if wid == 128:
                    e.tensor_copy(out=cb(PS + tc * 1024, TC),
                                  in_=pq.f32(128, TC)).then_inc(pq.done.h, 1)
                else:
                    e.tensor_copy(out=cf(PSD + tc * 2048, TC, 64),
                                  in_=pq.f32(64, TC)).then_inc(pq.done.h, 1)
                e.drain().then_inc(evs.h, 1)
            ctx.emit("vector", f_ev)
            pq.done.bump(1)
            pool.release(pq)
            evs.bump(1)

        def f_st(e, j=j, nev=evs.n):
            e.wait_ge(evs.h, nev)
            if j < 32:
                e.dma_start(out=kb.d_projG[j * 128:(j + 1) * 128, :],
                            in_=cb(PS, 4096)).then_inc(so.h, 16)
            elif j < 80:
                e.dma_start(out=kb.d_projH[(j - 32) * 128:(j - 31) * 128, :],
                            in_=cb(PS, 4096)).then_inc(so.h, 16)
            else:
                e.dma_start(out=kb.d_projD,
                            in_=cf(PSD, 4096, 64)).then_inc(so.h, 16)
        ctx.emit("sync", f_st)
        so.bump(16)

    ctx.pending.append((so, so.n))
    ctx.flush()


def _phase_B2(ctx, kb):
    cf, cb, ci = kb.cf, kb.cb, kb.ci
    pool = kb.pool

    GT = 0
    HB = 16384
    DTR = 41344
    CV = 42368
    HT = 66944
    BT = 83328
    ACM = 87424
    DTF = 88448
    ABT = 89472
    ONE1 = 90496
    AdT = 91520
    DDT = 92544
    LSB = 93056
    AEB = 93312
    CBM = 93568
    EM = 97664
    MT = 99712
    YB = 103808
    SA = 120192
    HSD = 152960
    CEX = 155008
    DIA = 159104
    EXA = 159360
    SEL = 160384
    CVS = 176768
    SGT = CVS
    SQ = CVS + 2048

    dL = ctx.sem(); cvv = ctx.sem(); cva = ctx.sem(); dtv = ctx.sem()
    tr = ctx.sem(); rowv = ctx.sem(); gmm = ctx.sem(); hv = ctx.sem()
    yv = ctx.sem(); sv = ctx.sem(); gv = ctx.sem(); stt_ = ctx.sem()
    yn = ctx.sem(); ck = ctx.sem(); gsel = ctx.sem(); ms = ctx.sem()
    cky = ctx.sem()
    prev = {"gv": 0, "cva": 0, "dt1": 0, "sv": 0, "yv": 0, "stt": 0, "ck": 0,
            "tr": 0, "hv": 0, "rowv": 0, "gmm": 0, "cvv": 0}

    def f_prep(e):
        for hh in range(32):
            reg = cf(SEL + hh * 512, 128, 64)
            e.memset(reg, 1.0)
            for half in range(2):
                sub = reg[half * 32:half * 32 + 32, :]
                e.drain()
                e.affine_select(out=sub, in_=sub, compare_op=ALU.is_ge, fill=0.0,
                                base=-hh, pattern=[[0, 128]], channel_multiplier=1)
                e.drain()
                e.affine_select(out=sub, in_=sub, compare_op=ALU.is_ge, fill=0.0,
                                base=hh, pattern=[[0, 128]], channel_multiplier=-1)
        e.memset(cf(ONE1, 256, 64), 1.0)
        e.drain().then_inc(gsel.h, 1)
    ctx.emit("gpsimd", f_prep)
    gsel.bump(1)
    for eng in ("tensor", "vector", "scalar", "sync"):
        ctx.emit(eng, lambda e, n=gsel.n: e.wait_ge(gsel.h, n))

    for g in range(16):
        seq = g // 8
        ci_ = g % 8
        T0 = seq * SEQ + ci_ * CH
        sp = g % 2

        lo = max(T0 - 1, seq * SEQ)
        hi = min(T0 + CH + 2, (seq + 1) * SEQ)
        ofs = lo - (T0 - 1)

        def f_ld(e, T0=T0, lo=lo, hi=hi, ofs=ofs, g=g,
                 ggate=prev["gv"], cgate=prev["cva"], dgate=prev["dt1"],
                 rgate=prev["rowv"]):
            if ggate > 0:
                e.wait_ge(gv.h, ggate)
            if cgate > 0:
                e.wait_ge(cva.h, cgate)
            if dgate > 0:
                e.wait_ge(dtv.h, dgate)
            e.dma_start(out=cb(GT, 32 * 256).rearrange("p (a t) -> p a t", t=256),
                        in_=kb.d_projG.rearrange("(a p) t -> p a t", p=128)
                        [:, :, T0:T0 + CH]).then_inc(dL.h, 16)
            e.dma_start(out=cb(HB, 48 * 260).rearrange("p (a t) -> p a t", t=260)
                        [:, :, ofs:ofs + (hi - lo)],
                        in_=kb.d_projH.rearrange("(a p) t -> p a t", p=128)
                        [:, :, lo:hi]).then_inc(dL.h, 16)
            e.dma_start(out=cf(DTR, 256, 64),
                        in_=kb.d_projD[:, T0:T0 + CH]).then_inc(dL.h, 16)
        ctx.emit("sync", f_ld)
        nld = dL.bump(48)

        def f_mz(e, g=g, sp=sp, ofs=ofs, hi=hi, lo=lo, nld=nld, svg=prev["sv"]):
            e.wait_ge(dL.h, nld)
            if g % 8 == 0:
                if svg > 0:
                    e.wait_ge(sv.h, svg)
                e.memset(cf(SA + sp * 16384, 4096), 0.0)
            if ofs > 0:
                e.memset(cb(HB, 48 * 260).rearrange("p (a t) -> p a t", t=260)
                         [:, :, 0:ofs], 0.0)
            tail = (CH + 3) - (ofs + (hi - lo))
            if tail > 0:
                e.memset(cb(HB, 48 * 260).rearrange("p (a t) -> p a t", t=260)
                         [:, :, ofs + (hi - lo):CH + 3], 0.0)
            e.drain().then_inc(ms.h, 1)
        ctx.emit("gpsimd", f_mz)
        nms = ms.bump(1)

        for cb48 in range(48):
            csl = cb48 % 2

            def f_cv(e, cb48=cb48, csl=csl, nld=nld, nms=nms, agate=cva.n - 1,
                     yg=prev["yv"], sg_=prev["sv"], cg=prev["stt"], gg=prev["gmm"],
                     cvg=prev["cvv"]):
                if cb48 == 0:
                    e.wait_ge(dL.h, nld)
                    e.wait_ge(ms.h, nms)
                    if yg > 0:
                        e.wait_ge(yv.h, yg)
                    if sg_ > 0:
                        e.wait_ge(sv.h, sg_)
                    if cg > 0:
                        e.wait_ge(stt_.h, cg)
                    if gg > 0:
                        e.wait_ge(gmm.h, gg)
                    if cvg > 0:
                        e.wait_ge(cvv.h, cvg)
                if agate > 0:
                    e.wait_ge(cva.h, agate)
                A0 = CVS + csl * 2048
                A1 = A0 + 1024
                hbrow = lambda k: cb(HB + cb48 * 520 + 2 * k, 256)
                e.tensor_scalar(out=cf(A0, 256), in0=hbrow(0),
                                scalar1=kb.sconvw[:, cb48 * 4:cb48 * 4 + 1],
                                scalar2=None, op0=ALU.mult)
                e.drain()
                e.scalar_tensor_tensor(out=cf(A1, 256), in0=hbrow(1),
                                       scalar=kb.sconvw[:, cb48 * 4 + 1:cb48 * 4 + 2],
                                       in1=cf(A0, 256), op0=ALU.mult, op1=ALU.add)
                e.drain()
                e.scalar_tensor_tensor(out=cf(A0, 256), in0=hbrow(2),
                                       scalar=kb.sconvw[:, cb48 * 4 + 2:cb48 * 4 + 3],
                                       in1=cf(A1, 256), op0=ALU.mult, op1=ALU.add)
                e.drain()
                e.scalar_tensor_tensor(out=cf(A1, 256), in0=hbrow(3),
                                       scalar=kb.sconvw[:, cb48 * 4 + 3:cb48 * 4 + 4],
                                       in1=cf(A0, 256), op0=ALU.mult,
                                       op1=ALU.add).then_inc(cvv.h, 1)
            ctx.emit("vector", f_cv)
            ncv = cvv.bump(1)

            def f_si(e, cb48=cb48, csl=csl, ncv=ncv):
                e.wait_ge(cvv.h, ncv)
                e.activation(out=cb(CV + cb48 * 512, 256),
                             in_=cf(CVS + csl * 2048 + 1024, 256),
                             func=AF.Silu,
                             bias=kb.sconvb[:, cb48:cb48 + 1]).then_inc(cva.h, 1)
            ctx.emit("scalar", f_si)
            cva.bump(1)
        ncva = cva.n

        def f_dt1(e, nld=nld, gg=prev["gmm"], rg=prev["rowv"], tg=prev["tr"]):
            e.wait_ge(dL.h, nld)
            if gg > 0:
                e.wait_ge(gmm.h, gg)
            if rg > 0:
                e.wait_ge(rowv.h, rg)
            if tg > 0:
                e.wait_ge(tr.h, tg)
            e.tensor_scalar(out=cf(DTF, 256, 64), in0=cf(DTR, 256, 64),
                            scalar1=kb.sDtb[:], scalar2=None,
                            op0=ALU.add).then_inc(dtv.h, 1)
        ctx.emit("vector", f_dt1)
        n1 = dtv.bump(1)

        def f_dt2(e, n1=n1):
            e.wait_ge(dtv.h, n1)
            e.activation(out=cf(ABT, 256, 64), in_=cf(DTF, 256, 64), func=AF.Abs)
            e.drain()
            e.activation(out=cf(ABT, 256, 64), in_=cf(ABT, 256, 64), func=AF.Exp,
                         scale=-1.0)
            e.drain()
            e.activation(out=cf(ABT, 256, 64), in_=cf(ABT, 256, 64), func=AF.Ln,
                         bias=kb.sconst[0:64, 2:3]).then_inc(dtv.h, 1)
        ctx.emit("scalar", f_dt2)
        n2 = dtv.bump(1)

        def f_dt3(e, n2=n2):
            e.wait_ge(dtv.h, n2)
            e.scalar_tensor_tensor(out=cf(DTF, 256, 64), in0=cf(DTF, 256, 64),
                                   scalar=0.0, in1=cf(ABT, 256, 64),
                                   op0=ALU.max, op1=ALU.add)
            e.drain()
            e.tensor_scalar(out=cf(ABT, 256, 64), in0=cf(DTF, 256, 64),
                            scalar1=kb.sAcol[:], scalar2=None, op0=ALU.mult)
            e.drain()
            e.tensor_tensor_scan(out=cf(ACM, 256, 64), data0=cf(ONE1, 256, 64),
                                 data1=cf(ABT, 256, 64), initial=0.0,
                                 op0=ALU.mult, op1=ALU.add)
            e.drain().then_inc(dtv.h, 1)
        ctx.emit("vector", f_dt3)
        n3 = dtv.bump(1)

        def f_exa(e, n3=n3):
            e.wait_ge(dtv.h, n3)
            e.activation(out=cf(EXA, 256, 64), in_=cf(ACM, 256, 64),
                         func=AF.Exp).then_inc(dtv.h, 1)
        ctx.emit("scalar", f_exa)
        n4 = dtv.bump(1)

        for lt in range(2):
            for (srcoff, dsto) in ((ACM, 0), (DTF, 64)):
                pt = pool.acquire(ctx)

                def f_tp(e, pt=pt, lt=lt, srcoff=srcoff, n3=n4):
                    e.wait_ge(dtv.h, n3)
                    e.transpose(out=pt.f32(128, 64),
                                in_=cf(srcoff + lt * 512, 128, 64),
                                identity=kb.ident_f[0:64, 0:64]).then_inc(tr.h, 1)
                ctx.emit("tensor", f_tp)
                ntr = tr.bump(1)

                def f_te(e, pt=pt, lt=lt, dsto=dsto, ntr=ntr):
                    e.wait_ge(tr.h, ntr)
                    e.tensor_copy(out=cf(AdT + lt * 512 + dsto * 4, 64),
                                  in_=pt.f32(128, 64)).then_inc(pt.done.h, 1)
                    e.drain().then_inc(rowv.h, 1)
                ctx.emit("vector", f_te)
                pt.done.bump(1)
                pool.release(pt)
                rowv.bump(1)

        pa = pool.acquire(ctx)

        def f_dia(e, n4=n4):
            e.wait_ge(dtv.h, n4)
            e.tensor_scalar(out=cf(DIA, 64, 64), in0=kb.ident_f[0:64, 0:64],
                            scalar1=cf(ACM + 255 * 4, 1, 64), scalar2=None,
                            op0=ALU.mult)
            e.drain().then_inc(rowv.h, 1)
        ctx.emit("vector", f_dia)
        n5 = rowv.bump(1)

        def f_ab(e, pa=pa, n5=n5):
            e.wait_ge(rowv.h, n5)
            e.matmul(out=pa.f32(128, 64), lhsT=kb.ones_f[0:64, :],
                     rhs=cf(DIA, 64, 64), start=True, stop=True).then_inc(tr.h, 1)
        ctx.emit("tensor", f_ab)
        ntr2 = tr.bump(1)

        def f_lam(e, pa=pa, ntr2=ntr2):
            e.wait_ge(tr.h, ntr2)
            e.activation(out=cf(LSB, 64), in_=pa.f32(128, 64),
                         func=AF.Exp).then_inc(rowv.h, 1)
        ctx.emit("scalar", f_lam)
        n6 = rowv.bump(1)

        def f_aeb(e, pa=pa, n6=n6):
            e.wait_ge(rowv.h, n6)
            e.tensor_copy(out=cf(AEB, 64), in_=pa.f32(128, 64)).then_inc(pa.done.h, 1)
            e.drain().then_inc(rowv.h, 1)
        ctx.emit("vector", f_aeb)
        pa.done.bump(1)
        pool.release(pa)
        n7 = rowv.bump(1)

        for lt in range(2):
            def f_ddt1(e, lt=lt, n7=n7, gate=rowv.n):
                if lt == 0:
                    e.wait_ge(rowv.h, n7)
                else:
                    e.wait_ge(rowv.h, gate)
                e.tensor_tensor(out=cf(EM, 64), in0=cf(AEB, 64),
                                in1=cf(AdT + lt * 512, 64),
                                op=ALU.subtract).then_inc(rowv.h, 1)
            ctx.emit("vector", f_ddt1)
            n8 = rowv.bump(1)

            def f_ddt2(e, lt=lt, n8=n8):
                e.wait_ge(rowv.h, n8)
                e.activation(out=cf(EM, 64), in_=cf(EM, 64),
                             func=AF.Exp).then_inc(rowv.h, 1)
            ctx.emit("scalar", f_ddt2)
            n9 = rowv.bump(1)

            def f_ddt3(e, lt=lt, n9=n9):
                e.wait_ge(rowv.h, n9)
                e.tensor_tensor(out=cf(DDT + lt * 256, 64), in0=cf(EM, 64),
                                in1=cf(AdT + lt * 512 + 256, 64),
                                op=ALU.mult).then_inc(rowv.h, 1)
            ctx.emit("vector", f_ddt3)
            rowv.bump(1)
        nrows = rowv.n

        for pr in range(16):
            for lt in range(2):
                pt = pool.acquire(ctx)

                def f_tp(e, pt=pt, pr=pr, lt=lt, ncva=ncva):
                    if pr == 0 and lt == 0:
                        e.wait_ge(cva.h, ncva)
                    e.transpose(out=pt.bf16(128, 128),
                                in_=cb(CV + pr * 2 * 512 + lt * 256, 128),
                                identity=kb.ident_bf[:]).then_inc(tr.h, 1)
                ctx.emit("tensor", f_tp)
                ntr = tr.bump(1)

                def f_te(e, pt=pt, pr=pr, lt=lt, ntr=ntr):
                    e.wait_ge(tr.h, ntr)
                    e.tensor_copy(out=cb(HT + lt * 8192 + pr * 256, 128),
                                  in_=pt.bf16(128, 128)).then_inc(pt.done.h, 1)
                    e.drain().then_inc(hv.h, 1)
                ctx.emit("vector", f_te)
                pt.done.bump(1)
                pool.release(pt)
                hv.bump(1)
        for g8 in range(8):
            for lt in range(2):
                pt = pool.acquire(ctx)

                def f_tp(e, pt=pt, g8=g8, lt=lt, ncva=ncva):
                    e.transpose(out=pt.bf16(128, 128),
                                in_=cb(CV + (32 + g8) * 512 + lt * 256, 128),
                                identity=kb.ident_bf[:]).then_inc(tr.h, 1)
                ctx.emit("tensor", f_tp)
                ntr = tr.bump(1)

                def f_te(e, pt=pt, g8=g8, lt=lt, ntr=ntr):
                    e.wait_ge(tr.h, ntr)
                    e.tensor_copy(out=cb(BT + (lt * 8 + g8) * 256, 128),
                                  in_=pt.bf16(128, 128)).then_inc(pt.done.h, 1)
                    e.drain().then_inc(hv.h, 1)
                ctx.emit("vector", f_te)
                pt.done.bump(1)
                pool.release(pt)
                hv.bump(1)
        nhv = hv.n

        for g8 in range(8):
            gs = g8 % 2
            for st_ in range(2):
                pc = pool.acquire(ctx)

                def f_cb(e, pc=pc, g8=g8, st_=st_, ncva=ncva):
                    if g8 == 0 and st_ == 0:
                        e.wait_ge(cva.h, ncva)
                    e.matmul(out=pc.f32(128, 256),
                             lhsT=cb(CV + (32 + g8) * 512 + st_ * 256, 128),
                             rhs=cb(CV + (40 + g8) * 512, 256),
                             start=True, stop=True).then_inc(gmm.h, 1)
                ctx.emit("tensor", f_cb)
                ngm = gmm.bump(1)

                def f_ms(e, pc=pc, g8=g8, gs=gs, st_=st_, ngm=ngm,
                         mgate=stt_.n - 2 * 16):
                    e.wait_ge(gmm.h, ngm)
                    if mgate > 0:
                        e.wait_ge(stt_.h, mgate)
                    e.tensor_tensor(out=cf(CBM + (gs * 2 + st_) * 1024, 256),
                                    in0=pc.f32(128, 256),
                                    in1=kb.tri[:, st_ * 256:(st_ + 1) * 256],
                                    op=ALU.mult).then_inc(pc.done.h, 1)
                    e.drain().then_inc(cvv.h, 1)
                ctx.emit("vector", f_ms)
                pc.done.bump(1)
                pool.release(pc)
                cvv.bump(1)
            ncbm = cvv.n

            for hh8 in range(8):
                h = g8 * 8 + hh8
                hrot = h % 4
                pr_a = pool.acquire(ctx)
                pr_e = pool.acquire(ctx)

                def f_sel(e, h=h, pr_a=pr_a, pr_e=pr_e, n4=n4, nrows=nrows,
                          first=(g8 == 0 and hh8 == 0), svg=prev["sv"]):
                    e.wait_ge(dtv.h, n4)
                    e.wait_ge(rowv.h, nrows)
                    if first and svg > 0:
                        e.wait_ge(sv.h, svg)
                    half = h // 32
                    sel = cf(SEL + (h % 32) * 512, 128, 64)[half * 32:half * 32 + 32, :]
                    e.matmul(out=pr_a.f32(128, 256), lhsT=sel,
                             rhs=cf(ACM, 256, 64)[half * 32:half * 32 + 32, :],
                             start=True, stop=True).then_inc(gmm.h, 1)
                    e.matmul(out=pr_e.f32(128, 256), lhsT=sel,
                             rhs=cf(EXA, 256, 64)[half * 32:half * 32 + 32, :],
                             start=True, stop=True).then_inc(gmm.h, 1)
                ctx.emit("tensor", f_sel)
                nga = gmm.bump(1)
                nge = gmm.bump(1)

                for st_ in range(2):
                    def f_e1(e, pr_a=pr_a, h=h, st_=st_, nga=nga, hrot=hrot,
                             mgate=gv.n - 2 * 2):
                        if st_ == 0:
                            e.wait_ge(gmm.h, nga)
                        if mgate > 0:
                            e.wait_ge(gv.h, mgate)
                        e.tensor_scalar(out=cf(EM + st_ * 1024, 256),
                                        in0=pr_a.f32(128, 256),
                                        scalar1=cf(AdT + st_ * 512 + h * 4, 1, 128)
                                        if False else cf(AdT + st_ * 512 + h * 4, 1),
                                        scalar2=0.0, op0=ALU.subtract,
                                        op1=ALU.min).then_inc(stt_.h, 1)
                    ctx.emit("vector", f_e1)
                    ne1 = stt_.bump(1)

                    def f_e2(e, st_=st_, ne1=ne1):
                        e.wait_ge(stt_.h, ne1)
                        e.activation(out=cf(EM + st_ * 1024, 256),
                                     in_=cf(EM + st_ * 1024, 256),
                                     func=AF.Exp).then_inc(stt_.h, 1)
                    ctx.emit("scalar", f_e2)
                    ne2 = stt_.bump(1)

                    def f_m(e, st_=st_, gs=gs, h=h, hrot=hrot, ne2=ne2):
                        e.wait_ge(stt_.h, ne2)
                        e.scalar_tensor_tensor(out=cb(MT + (hrot * 2 + st_) * 512, 256),
                                               in0=cf(CBM + (gs * 2 + st_) * 1024, 256),
                                               scalar=cf(AdT + st_ * 512 + 256 + h * 4, 1),
                                               in1=cf(EM + st_ * 1024, 256),
                                               op0=ALU.mult,
                                               op1=ALU.mult).then_inc(stt_.h, 1)
                    ctx.emit("vector", f_m)
                    stt_.bump(1)
                nmt = stt_.n

                def f_cx(e, g8=g8, hrot=hrot, pr_e=pr_e, nge=nge,
                         cgate=sv.n - 4):
                    e.wait_ge(gmm.h, nge)
                    if cgate > 0:
                        e.wait_ge(sv.h, cgate)
                    e.tensor_tensor(out=cf(CEX + hrot * 1024, 256),
                                    in0=cb(CV + (40 + g8) * 512, 256),
                                    in1=pr_e.f32(128, 256),
                                    op=ALU.mult).then_inc(pr_e.done.h, 1)
                    e.drain().then_inc(stt_.h, 1)
                ctx.emit("vector", f_cx)
                pr_e.done.bump(1)
                pool.release(pr_e)
                ncx = stt_.bump(1)

                py = pool.acquire(ctx)
                hbase = 64 * (h % 2)

                def f_y(e, h=h, hrot=hrot, py=py, sp=sp, nmt=nmt, ncx=ncx,
                        hbase=hbase, pr_a=pr_a):
                    e.wait_ge(stt_.h, max(nmt, ncx))
                    for st_ in range(2):
                        e.matmul(out=py.f32(128, 256)[hbase:hbase + 64, :],
                                 lhsT=cb(HT + st_ * 8192 + h * 128, 64),
                                 rhs=cb(MT + (hrot * 2 + st_) * 512, 256),
                                 start=(st_ == 0), stop=False)
                    m = e.matmul(out=py.f32(128, 256)[hbase:hbase + 64, :],
                                 lhsT=cf(SA + sp * 16384 + h * 256, 64),
                                 rhs=cf(CEX + hrot * 1024, 256),
                                 start=False, stop=True)
                    m.then_inc(gmm.h, 1)
                    e.nop().then_inc(pr_a.done.h, 1)
                ctx.emit("tensor", f_y)
                pr_a.done.bump(1)
                pool.release(pr_a)
                ny = gmm.bump(1)

                def f_ye(e, h=h, py=py, ny=ny, hbase=hbase,
                         first=(h == 0), ckg=prev["ck"]):
                    e.wait_ge(gmm.h, ny)
                    if first and ckg > 0:
                        e.wait_ge(cky.h, ckg)
                    e.scalar_tensor_tensor(
                        out=cb(YB + (h // 2) * 512, 256)[hbase:hbase + 64, :],
                        in0=cb(CV + (h // 2) * 512, 256)[hbase:hbase + 64, :],
                        scalar=kb.sDbc[hbase:hbase + 64, h:h + 1],
                        in1=py.f32(128, 256)[hbase:hbase + 64, :],
                        op0=ALU.mult, op1=ALU.add).then_inc(py.done.h, 1)
                    e.drain().then_inc(yv.h, 1)
                ctx.emit("vector", f_ye)
                py.done.bump(1)
                pool.release(py)
                yv.bump(1)

                for lt in range(2):
                    def f_hd(e, h=h, lt=lt, hrot=hrot, nhv=nhv, nrows=nrows,
                             hgate=sv.n - 4):
                        if lt == 0:
                            e.wait_ge(hv.h, nhv)
                            e.wait_ge(rowv.h, nrows)
                            if hgate > 0:
                                e.wait_ge(sv.h, hgate)
                        e.tensor_scalar(out=cb(HSD + (hrot * 2 + lt) * 256, 128)
                                        [:, 0:64],
                                        in0=cb(HT + lt * 8192 + h * 128, 64),
                                        scalar1=cf(DDT + lt * 256 + h * 4, 1),
                                        scalar2=None,
                                        op0=ALU.mult).then_inc(stt_.h, 1)
                    ctx.emit("vector", f_hd)
                    stt_.bump(1)
                nhsd = stt_.n
                ps_ = pool.acquire(ctx)

                def f_sc(e, h=h, g8=g8, hrot=hrot, ps_=ps_, nhsd=nhsd):
                    e.wait_ge(stt_.h, nhsd)
                    for lt in range(2):
                        m = e.matmul(out=ps_.f32(128, 64),
                                     lhsT=cb(BT + (lt * 8 + g8) * 256, 128),
                                     rhs=cb(HSD + (hrot * 2 + lt) * 256, 128)[:, 0:64],
                                     start=(lt == 0), stop=(lt == 1))
                    m.then_inc(gmm.h, 1)
                ctx.emit("tensor", f_sc)
                nsc = gmm.bump(1)

                def f_su(e, h=h, sp=sp, ps_=ps_, nsc=nsc):
                    e.wait_ge(gmm.h, nsc)
                    e.scalar_tensor_tensor(out=cf(SA + (1 - sp) * 16384 + h * 256, 64),
                                           in0=cf(SA + sp * 16384 + h * 256, 64),
                                           scalar=cf(LSB + h * 4, 1),
                                           in1=ps_.f32(128, 64),
                                           op0=ALU.mult, op1=ALU.add).then_inc(ps_.done.h, 1)
                    e.drain().then_inc(sv.h, 1)
                ctx.emit("vector", f_su)
                ps_.done.bump(1)
                pool.release(ps_)
                sv.bump(1)
        nyall = yv.n
        nsall = sv.n

        pq1 = pool.acquire(ctx)
        pq2 = pool.acquire(ctx)
        for ib in range(32):
            sl = ib % 2

            def f_sg(e, ib=ib, sl=sl, nld=nld, nyall=nyall, ygate=gv.n - 2):
                if ib == 0:
                    e.wait_ge(dL.h, nld)
                if ygate > 0:
                    e.wait_ge(gv.h, ygate)
                e.activation(out=cf(SGT + sl * 1024, 256),
                             in_=cb(GT + ib * 512, 256),
                             func=AF.Silu).then_inc(gv.h, 1)
            ctx.emit("scalar", f_sg)
            nsg = gv.bump(1)

            def f_yf(e, ib=ib, sl=sl, nsg=nsg, nyall=nyall):
                e.wait_ge(gv.h, nsg)
                if ib == 0:
                    e.wait_ge(yv.h, nyall)
                e.tensor_tensor(out=cb(YB + ib * 512, 256),
                                in0=cb(YB + ib * 512, 256),
                                in1=cf(SGT + sl * 1024, 256),
                                op=ALU.mult).then_inc(gv.h, 1)
            ctx.emit("vector", f_yf)
            nyf = gv.bump(1)

            def f_sq(e, ib=ib, sl=sl, nyf=nyf, sgate=gmm.n - 1):
                e.wait_ge(gv.h, nyf)
                if sgate > 0:
                    e.wait_ge(gmm.h, sgate)
                e.activation(out=cf(SQ + sl * 1024, 256),
                             in_=cb(YB + ib * 512, 256),
                             func=AF.Square).then_inc(gv.h, 1)
            ctx.emit("scalar", f_sq)
            nsq2 = gv.bump(1)

            def f_st(e, ib=ib, sl=sl, pq1=pq1, nsq2=nsq2):
                e.wait_ge(gv.h, nsq2)
                e.matmul(out=pq1.f32(1, 256), lhsT=kb.ones_f[:, 0:1],
                         rhs=cf(SQ + sl * 1024, 256), start=(ib == 0),
                         stop=(ib == 31)).then_inc(gmm.h, 1)
            ctx.emit("tensor", f_st)
            gmm.bump(1)
        nstats = gmm.n

        def f_rs(e, pq1=pq1, nstats=nstats):
            e.wait_ge(gmm.h, nstats)
            e.tensor_scalar(out=kb.srow[0:1, 0:256], in0=pq1.f32(1, 256),
                            scalar1=1.0 / INTER, scalar2=None,
                            op0=ALU.mult).then_inc(pq1.done.h, 1)
            e.drain().then_inc(yn.h, 1)
        ctx.emit("vector", f_rs)
        pq1.done.bump(1)
        pool.release(pq1)
        nr1 = yn.bump(1)

        def f_rs2(e, nr1=nr1):
            e.wait_ge(yn.h, nr1)
            e.activation(out=kb.srow[0:1, 0:256], in_=kb.srow[0:1, 0:256],
                         func=AF.Sqrt, bias=kb.sconst[0:1, 1:2]).then_inc(yn.h, 1)
        ctx.emit("scalar", f_rs2)
        nr2 = yn.bump(1)

        def f_rs3(e, nr2=nr2):
            e.wait_ge(yn.h, nr2)
            e.reciprocal(out=kb.srow[0:1, 0:256],
                         in_=kb.srow[0:1, 0:256]).then_inc(yn.h, 1)
        ctx.emit("vector", f_rs3)
        nr3 = yn.bump(1)

        def f_rb(e, pq2=pq2, nr3=nr3):
            e.wait_ge(yn.h, nr3)
            e.matmul(out=pq2.f32(128, 256), lhsT=kb.ones_f[0:1, :],
                     rhs=kb.srow[0:1, 0:256], start=True, stop=True).then_inc(gmm.h, 1)
        ctx.emit("tensor", f_rb)
        nrb = gmm.bump(1)

        for ib in range(32):
            def f_yn(e, ib=ib, pq2=pq2, nrb=nrb):
                if ib == 0:
                    e.wait_ge(gmm.h, nrb)
                m = e.scalar_tensor_tensor(out=cb(YB + ib * 512, 256),
                                           in0=cb(YB + ib * 512, 256),
                                           scalar=kb.snormw[:, ib:ib + 1],
                                           in1=pq2.f32(128, 256),
                                           op0=ALU.mult, op1=ALU.mult)
                if ib == 31:
                    m.then_inc(pq2.done.h, 1)
                e.drain().then_inc(yn.h, 1)
            ctx.emit("vector", f_yn)
            yn.bump(1)
        pq2.done.bump(1)
        pool.release(pq2)
        nynd = yn.n

        def f_sy(e, T0=T0, nynd=nynd):
            e.wait_ge(yn.h, nynd)
            e.dma_start(out=kb.d_ynT.rearrange("(a p) t -> p a t", p=128)
                        [:, :, T0:T0 + CH],
                        in_=cb(YB, 32 * 256).rearrange("p (a t) -> p a t", t=256)
                        ).then_inc(cky.h, 16)
        ctx.emit("sync", f_sy)
        nck1 = cky.bump(16)

        def f_ckv(e, nsall=nsall):
            e.wait_ge(sv.h, nsall)
            e.drain().then_inc(ck.h, 1)
        ctx.emit("vector", f_ckv)
        ck.bump(1)
        prev = {"gv": gv.n, "cva": cva.n, "dt1": dtv.n, "sv": sv.n,
                "yv": yv.n, "stt": stt_.n, "ck": nck1, "tr": tr.n,
                "hv": hv.n, "rowv": rowv.n, "gmm": gmm.n, "cvv": cvv.n}

    ctx.pending.append((ck, ck.n))
    ctx.pending.append((cky, cky.n))
    ctx.flush()


def _phase_B3(ctx, kb):
    cf, cb = kb.cf, kb.cb
    pool = kb.pool

    OUTW = 0
    YS = 131072
    X2S = 163840
    X3S = 167936

    dW = ctx.sem(); dY = ctx.sem(); mm = ctx.sem(); ev = ctx.sem()
    dx2s = [ctx.sem(), ctx.sem()]
    sos = [ctx.sem(), ctx.sem()]

    def f_lw(e):
        for ib in range(32):
            e.dma_start(out=cb(OUTW + ib * 4096, 2048),
                        in_=kb.d_outw[ib * 128:(ib + 1) * 128, :]).then_inc(dW.h, 16)
    ctx.emit("gpsimd", f_lw)
    nW = dW.bump(16 * 32)

    for tc in range(NTC):
        def f_ly(e, tc=tc, gate=mm.n):
            if gate > 0:
                e.wait_ge(mm.h, gate)
            e.dma_start(out=sb3(cb(YS, 32 * TC)),
                        in_=kb.d_ynT.rearrange("k (c t) -> k c t", t=TC)[:, tc, :]
                        .rearrange("(a p) t -> p a t", p=128)).then_inc(dY.h, 16)
        ctx.emit("sync", f_ly)
        ndy = dY.bump(16)

        for j in range(NKC):
            slot = j % 2
            pq = pool.acquire(ctx)

            def f_lx(e, j=j, tc=tc, slot=slot, egate=ev.n - 1):
                if egate > 0:
                    e.wait_ge(ev.h, egate)
                e.dma_start(out=cf(X2S + slot * 2048, 512),
                            in_=kb.d_x2.rearrange("k (c t) -> k c t", t=TC)
                            [j * 128:(j + 1) * 128, tc, :]).then_inc(dx2s[slot].h, 16)
            ctx.emit("sync", f_lx)
            ndx = dx2s[slot].bump(16)

            def f_mm(e, j=j, tc=tc, pq=pq, ndy=ndy, nW=nW, first=(j == 0 and tc == 0)):
                if j == 0:
                    e.wait_ge(dY.h, ndy)
                if first:
                    e.wait_ge(dW.h, nW)
                for ib in range(32):
                    m = e.matmul(out=pq.f32(128, TC),
                                 lhsT=cb(OUTW + ib * 4096 + j * 256, 128),
                                 rhs=cb(YS + ib * 1024, TC),
                                 start=(ib == 0), stop=(ib == 31))
                m.then_inc(mm.h, 1)
            ctx.emit("tensor", f_mm)
            nmm = mm.bump(1)

            def f_ev(e, j=j, slot=slot, pq=pq, nmm=nmm, ndx=ndx,
                     sgate=sos[slot].n):
                e.wait_ge(mm.h, nmm)
                e.wait_ge(dx2s[slot].h, ndx)
                if sgate > 0:
                    e.wait_ge(sos[slot].h, sgate)
                e.tensor_tensor(out=cf(X3S + slot * 2048, 512),
                                in0=cf(X2S + slot * 2048, 512),
                                in1=pq.f32(128, TC), op=ALU.add).then_inc(pq.done.h, 1)
                e.drain().then_inc(ev.h, 1)
            ctx.emit("vector", f_ev)
            pq.done.bump(1)
            pool.release(pq)
            nev = ev.bump(1)

            def f_sx(e, j=j, tc=tc, slot=slot, nev=nev):
                e.wait_ge(ev.h, nev)
                e.dma_start(out=kb.d_x3.rearrange("k (c t) -> k c t", t=TC)
                            [j * 128:(j + 1) * 128, tc, :],
                            in_=cf(X3S + slot * 2048, 512)).then_inc(sos[slot].h, 16)
            ctx.emit("sync", f_sx)
            sos[slot].bump(16)

    ctx.pending.append((sos[0], sos[0].n))
    ctx.pending.append((sos[1], sos[1].n))
    ctx.flush()


def _host_maps(inputs):
    sig = _parity_perm(HID)
    tau = _parity_perm(MLP)
    f32 = np.float32

    Wq = np.asarray(inputs["L0_attn_a_Wq"], f32)
    Wo = np.asarray(inputs["L0_attn_a_Wo"], f32)
    mem = np.asarray(inputs["L0_attn_a_memories"], f32)
    mn = np.asarray(inputs["L0_attn_a_memory_norms"], f32)
    base = {
        "wqT": np.ascontiguousarray(Wq.T[sig]),
        "woT": np.ascontiguousarray(Wo.T[:, sig]),
        "memT": np.ascontiguousarray(mem.transpose(2, 0, 1, 3).reshape(128, -1)),
        "mnT": np.ascontiguousarray(mn.transpose(2, 1, 0).reshape(128, -1)),
        "bankw": np.ascontiguousarray(np.asarray(inputs["L0_attn_a_bank_weights"], f32)),
        "mnrows": np.ascontiguousarray(mn.reshape(NBANK, -1)),
        "inw": np.ascontiguousarray(np.asarray(inputs["L0_attn_b_in_proj_w"], f32)[sig]),
        "outw": np.ascontiguousarray(np.asarray(inputs["L0_attn_b_out_proj_w"], f32)[:, sig]),
        "convw": np.ascontiguousarray(
            np.asarray(inputs["L0_attn_b_conv_w"], f32).reshape(4, CONVD).T
            .reshape(48, 128, 4).transpose(1, 0, 2).reshape(128, 192)),
        "convb": np.ascontiguousarray(
            np.asarray(inputs["L0_attn_b_conv_b"], f32).reshape(48, 128).T),
        "dtb": np.ascontiguousarray(np.asarray(inputs["L0_attn_b_dt_bias"], f32).reshape(NH, 1)),
        "alog": np.ascontiguousarray(np.asarray(inputs["L0_attn_b_A_log"], f32).reshape(NH, 1)),
        "dvec": np.ascontiguousarray(np.asarray(inputs["L0_attn_b_D"], f32).reshape(NH, 1)),
        "normw": np.ascontiguousarray(
            np.asarray(inputs["L0_attn_b_norm_w"], f32).reshape(32, 128).T),
        "gpT": np.ascontiguousarray(np.asarray(inputs["L0_ffn_gate_W_packed"], np.int32)[tau].T),
        "gsT": np.ascontiguousarray(np.asarray(inputs["L0_ffn_gate_scales"], f32)[tau].T),
        "upT": np.ascontiguousarray(np.asarray(inputs["L0_ffn_up_W_packed"], np.int32)[tau].T),
        "usT": np.ascontiguousarray(np.asarray(inputs["L0_ffn_up_scales"], f32)[tau].T),
        "dpT": np.ascontiguousarray(np.asarray(inputs["L0_ffn_down_W_packed"], np.int32)[sig].T),
        "dsT": np.ascontiguousarray(np.asarray(inputs["L0_ffn_down_scales"], f32)[sig].T),
    }
    x = np.asarray(inputs["L0_attn_a_hidden_states"], f32)
    maps = []
    for c in range(8):
        m = dict(base)
        m["xt"] = np.ascontiguousarray(
            x[2 * c:2 * c + 2].transpose(2, 0, 1).reshape(HID, TCORE)[sig])
        maps.append(m)
    return maps, sig


def kernel(**inputs):
    maps, sig = _host_maps(inputs)
    nc = build(upto=8)
    res = run_bass_kernel_spmd(nc, maps, core_ids=list(range(8)))
    out = np.empty((16, SEQ, HID), np.float32)
    for c in range(8):
        t = np.empty((HID, TCORE), np.float32)
        t[sig] = res.results[c]["oxt"]
        out[2 * c:2 * c + 2] = t.reshape(HID, 2, SEQ).transpose(1, 2, 0)
    return out


# version 04 (final)
# speedup vs baseline: 1.0650x; 1.0650x over previous
import numpy as np
import concourse.bass as bass
import concourse.mybir as mybir
from concourse.bass_utils import run_bass_kernel_spmd

F32 = mybir.dt.float32
I32 = mybir.dt.int32
BF16 = mybir.dt.bfloat16
AF = mybir.ActivationFunctionType
ALU = mybir.AluOpType
AX = mybir.AxisListType

HID = 2048
TCORE = 4096
NKC = 16
TC = 512
NTC = 8
NHEAD_A = 16
NBANK = 4
MLP = 16384
PROJ = 10304
INTER = 4096
CONVD = 6144
NH = 64
HD = 64
NG = 8
SS = 128
CH = 256
SEQ = 2048
EPS_LN = 1e-5
EPS_A = 1e-6
EPS_RMS = 1e-6

ENGS = ("sync", "gpsimd", "vector", "scalar", "tensor")


def _parity_perm(n):
    return np.arange(n).reshape(-1, 128, 2).transpose(0, 2, 1).reshape(-1)


class Sem:
    def __init__(self, nc, name):
        self.h = nc.alloc_semaphore(name)
        self.n = 0

    def bump(self, k=1):
        self.n += k
        return self.n


class Ctx:
    def __init__(self, nc, block):
        self.nc = nc
        self.block = block
        self.ops = {e: [] for e in ENGS}
        self.pending = []
        self.bar = Sem(nc, "bar")
        self.nsem = 0

    def sem(self, name=None):
        self.nsem += 1
        return Sem(self.nc, name or f"s{self.nsem}")

    def sem_cached(self, key):
        if not hasattr(self, "_semcache"):
            self._semcache = {}
        if key not in self._semcache:
            self._semcache[key] = self.sem(key)
        return self._semcache[key]

    def emit(self, eng, fn):
        self.ops[eng].append(fn)

    def flush(self):
        for e in ENGS:
            fns = self.ops[e]
            if not fns:
                continue

            def body(engine, fns=tuple(fns)):
                for f in fns:
                    f(engine)
            getattr(self.block, e)(body)
            self.ops[e] = []

    def barrier(self):
        pend = list(self.pending)
        self.pending = []
        self.bar.bump(5)
        k = self.bar.n

        def f_sync(e, pend=pend, k=k):
            for (s, v) in pend:
                e.wait_ge(s.h, v)
            e.sem_inc(self.bar.h, 1)
            e.wait_ge(self.bar.h, k)
        self.emit("sync", f_sync)
        for eng in ("tensor", "vector", "scalar", "gpsimd"):
            def f_e(e, k=k):
                e.drain().then_inc(self.bar.h, 1)
                e.wait_ge(self.bar.h, k)
            self.emit(eng, f_e)
        self.flush()


class Bank:
    def __init__(self, ctx, i):
        self.ap = ctx.nc.alloc_psum_tensor(f"bank{i}", [128, 512], F32)
        self.done = ctx.sem(f"bkd{i}")
        self.uses = 0

    def f32(self, p=128, f=512):
        return self.ap[0:p, 0:f]

    def bf16(self, p=128, f=512):
        return self.ap[:].bitcast(BF16)[0:p, 0:f]


class BankPool:
    def __init__(self, ctx, n=8):
        self.banks = [Bank(ctx, i) for i in range(n)]
        self.free = list(self.banks)

    def acquire(self, ctx, prod_eng="tensor"):
        b = self.free.pop(0)
        need = b.uses
        b.uses += 1
        if need > 0:
            ctx.emit(prod_eng, lambda e, b=b, need=need: e.wait_ge(b.done.h, need))
        return b

    def release(self, b):
        self.free.append(b)


class KB:
    pass


def chunk_ap(d, c, tc=TC):
    return (d.rearrange("k (c t) -> k c t", t=tc)[:, c, :]
             .rearrange("(a p) t -> p a t", p=128))


def sb3(ap, tc=TC):
    return ap.rearrange("p (a t) -> p a t", t=tc)


def build(upto=8, dbg=False):
    nc = bass.Bass("TRN2", target_bir_lowering=False, debug=False)
    kb = KB()
    kb.nc = nc

    def din(name, shape, dt=F32):
        return nc.dram_tensor(name, shape, dt, kind="ExternalInput").ap()

    def dscr(name, shape, dt, out=False):
        kind = "ExternalOutput" if out else "Internal"
        return nc.dram_tensor(name, shape, dt, kind=kind).ap()

    kb.d_xt = din("xt", [HID, TCORE])
    kb.d_wqT = din("wqT", [HID, HID])
    kb.d_woT = din("woT", [HID, HID])
    kb.d_memT = din("memT", [128, NBANK * NHEAD_A * 128])
    kb.d_mnT = din("mnT", [128, NHEAD_A * NBANK])
    kb.d_bankw = din("bankw", [NHEAD_A, NBANK])
    kb.d_mnrows = din("mnrows", [NBANK, HID])
    kb.d_inw = din("inw", [HID, PROJ])
    kb.d_outw = din("outw", [INTER, HID])
    kb.d_convw = din("convw", [128, 48 * 4])
    kb.d_convb = din("convb", [128, 48])
    kb.d_dtb = din("dtb", [NH, 1])
    kb.d_alog = din("alog", [NH, 1])
    kb.d_dvec = din("dvec", [NH, 1])
    kb.d_normw = din("normw", [128, 32])
    kb.d_gpT = din("gpT", [1024, MLP], I32)
    kb.d_gsT = din("gsT", [64, MLP])
    kb.d_upT = din("upT", [1024, MLP], I32)
    kb.d_usT = din("usT", [64, MLP])
    kb.d_dpT = din("dpT", [MLP // 2, HID], I32)
    kb.d_dsT = din("dsT", [MLP // 32, HID])

    kb.d_x1 = dscr("x1", [HID, TCORE], F32, out=(dbg and upto == 1))
    kb.d_comb = dscr("comb", [HID, TCORE], BF16, out=(dbg and upto == 1))
    kb.d_x2 = dscr("x2", [HID, TCORE], F32, out=(dbg and upto == 2))
    kb.d_inter = dscr("inter", [MLP, TCORE], BF16, out=(dbg and upto == 15))
    kb.d_wgC = dscr("wgC", [64, HID, 256], BF16)
    kb.d_wuC = dscr("wuC", [64, HID, 256], BF16)
    kb.d_wdC = dscr("wdC", [4, MLP, TC], BF16)
    kb.d_projG = dscr("projG", [INTER, TCORE], BF16, out=(dbg and upto == 25))
    kb.d_projH = dscr("projH", [CONVD, TCORE], BF16)
    kb.d_projD = dscr("projD", [NH, TCORE], F32)
    kb.d_ynT = dscr("ynT", [INTER, TCORE], BF16, out=(dbg and upto == 26))
    kb.d_x3 = dscr("x3", [HID, TCORE], F32, out=(dbg and upto == 3))
    kb.d_out = nc.dram_tensor("oxt", [HID, TCORE], F32, kind="ExternalOutput").ap()
    kb.dbg = dbg
    if dbg:
        kb.d_dbgb = dscr("dbgb", [HID, TCORE], BF16, out=True)
        kb.d_dbgr = dscr("dbgr", [16, 2048], F32, out=True)
        kb.d_dbgs = dscr("dbgs", [16, 2048], F32, out=True)

    kb.ident_bf = nc.alloc_sbuf_tensor("ident_bf", [128, 128], BF16)
    kb.ident_f = nc.alloc_sbuf_tensor("ident_f", [128, 128], F32)
    kb.ones_f = nc.alloc_sbuf_tensor("ones_f", [128, 128], F32)
    kb.epat = nc.alloc_sbuf_tensor("epat", [8, 128], F32)
    kb.tri = nc.alloc_sbuf_tensor("tri", [128, 512], F32)
    kb.wact = nc.alloc_sbuf_tensor("wact", [NBANK, NHEAD_A], F32)
    kb.sconvw = nc.alloc_sbuf_tensor("sconvw", [128, 192], F32)
    kb.sconvb = nc.alloc_sbuf_tensor("sconvb", [128, 48], F32)
    kb.snormw = nc.alloc_sbuf_tensor("snormw", [128, 32], F32)
    kb.sAcol = nc.alloc_sbuf_tensor("sAcol", [NH, 1], F32)
    kb.sDtb = nc.alloc_sbuf_tensor("sDtb", [NH, 1], F32)
    kb.sDbc = nc.alloc_sbuf_tensor("sDbc", [128, NH], F32)
    kb.srow = nc.alloc_sbuf_tensor("srow", [16, 2048], F32)
    kb.sconst = nc.alloc_sbuf_tensor("sconst", [128, 4], F32)

    AR_BYTES = 184320
    kb.AR = nc.alloc_sbuf_tensor("AR", [128, AR_BYTES // 4], F32)

    def cf(off, n, p=128):
        assert off % 4 == 0 and off + 4 * n <= AR_BYTES, (off, n)
        return kb.AR[0:p, off // 4:off // 4 + n]

    def cb(off, n, p=128):
        assert off % 2 == 0 and off + 2 * n <= AR_BYTES, (off, n)
        return kb.AR[:].bitcast(BF16)[0:p, off // 2:off // 2 + n]

    def ci(off, n, p=128):
        assert off % 4 == 0 and off + 4 * n <= AR_BYTES, (off, n)
        return kb.AR[:].bitcast(I32)[0:p, off // 4:off // 4 + n]

    kb.cf, kb.cb, kb.ci = cf, cb, ci

    ctx = Ctx(nc, None)
    kb.pool = BankPool(ctx, 8)

    with nc.Block() as block:
        ctx.block = block
        _phase_prep(ctx, kb)
        ctx.barrier()
        _phase_A1(ctx, kb)
        ctx.barrier()
        _phase_A2(ctx, kb)
        ctx.barrier()
        if upto >= 2:
            _phase_Fa(ctx, kb, call=1)
            ctx.barrier()
            _phase_Fb(ctx, kb, call=1)
            ctx.barrier()
        if upto >= 3:
            _phase_B1(ctx, kb)
            ctx.barrier()
            _phase_B2(ctx, kb)
            ctx.barrier()
            _phase_B3(ctx, kb)
            ctx.barrier()
        if upto >= 4:
            _phase_Fa(ctx, kb, call=2)
            ctx.barrier()
            _phase_Fb(ctx, kb, call=2)
            ctx.barrier()
        pend = list(ctx.pending)

        def fin(e, pend=pend):
            for (s, v) in pend:
                e.wait_ge(s.h, v)
        ctx.emit("sync", fin)
        ctx.flush()
    return nc


def _phase_prep(ctx, kb):
    nc = ctx.nc
    cf = kb.cf
    pool = kb.pool
    d = ctx.sem("prepd")
    g = ctx.sem("prepg")
    v = ctx.sem("prepv")
    t = ctx.sem("prept")

    BW, MR, SM, DP = 0, 1024, 16384, 20480
    loads = [(cf(BW, 4, 16), kb.d_bankw), (cf(MR, 2048, 4), kb.d_mnrows),
             (kb.sconvw[:], kb.d_convw), (kb.sconvb[:], kb.d_convb),
             (kb.snormw[:], kb.d_normw), (kb.sDtb[:], kb.d_dtb),
             (kb.sAcol[:], kb.d_alog), (cf(DP, 1, NH), kb.d_dvec)]

    def f_sync(e):
        for (dst, src) in loads:
            e.dma_start(out=dst, in_=src).then_inc(d.h, 16)
    ctx.emit("sync", f_sync)
    nload = d.bump(16 * len(loads))

    def f_g(e):
        e.memset(kb.ones_f[:], 1.0)
        for ident in (kb.ident_bf, kb.ident_f):
            e.memset(ident[:], 1.0)
            e.drain()
            e.affine_select(out=ident[:], in_=ident[:], compare_op=ALU.is_ge,
                            fill=0.0, base=0, pattern=[[-1, 128]], channel_multiplier=1)
            e.drain()
            e.affine_select(out=ident[:], in_=ident[:], compare_op=ALU.is_ge,
                            fill=0.0, base=0, pattern=[[1, 128]], channel_multiplier=-1)
        e.memset(kb.tri[:], 1.0)
        e.drain()
        e.affine_select(out=kb.tri[:, 0:256], in_=kb.tri[:, 0:256], compare_op=ALU.is_ge,
                        fill=0.0, base=0, pattern=[[1, 256]], channel_multiplier=-1)
        e.affine_select(out=kb.tri[:, 256:512], in_=kb.tri[:, 256:512], compare_op=ALU.is_ge,
                        fill=0.0, base=-128, pattern=[[1, 256]], channel_multiplier=-1)
        e.memset(kb.sconst[:, 0:1], EPS_LN)
        e.memset(kb.sconst[:, 1:2], EPS_RMS)
        e.memset(kb.sconst[:, 2:3], 1.0)
        e.memset(kb.sconst[:, 3:4], 0.0)
        e.memset(kb.epat[:], 1.0)
        e.drain()
        e.affine_select(out=kb.epat[:], in_=kb.epat[:], compare_op=ALU.is_ge,
                        fill=0.0, base=0, pattern=[[1, 128]], channel_multiplier=-16)
        e.drain()
        e.affine_select(out=kb.epat[:], in_=kb.epat[:], compare_op=ALU.is_ge,
                        fill=0.0, base=15, pattern=[[-1, 128]], channel_multiplier=16)
        e.wait_ge(d.h, nload)
        e.memset(cf(DP + 4, 63, NH), 0.0).then_inc(g.h, 1)
    ctx.emit("gpsimd", f_g)
    ng = g.bump(1)

    def f_v(e):
        e.wait_ge(d.h, nload)
        e.reduce_max(kb.srow[0:16, 0:1], cf(BW, 4, 16), axis=AX.X).then_inc(v.h, 1)
        e.wait_ge(v.h, 1)
        e.tensor_scalar(out=cf(SM, 4, 16), in0=cf(BW, 4, 16),
                        scalar1=kb.srow[0:16, 0:1], scalar2=None,
                        op0=ALU.subtract).then_inc(v.h, 1)
    ctx.emit("vector", f_v)
    v.bump(1)
    nv1 = v.bump(1)

    def f_s(e):
        e.wait_ge(v.h, nv1)
        e.activation(out=cf(SM, 4, 16), in_=cf(SM, 4, 16), func=AF.Exp)
        e.activation(out=kb.sAcol[:], in_=kb.sAcol[:], func=AF.Exp).then_inc(v.h, 1)
    ctx.emit("scalar", f_s)
    nv2 = v.bump(1)

    def f_v2(e):
        e.wait_ge(v.h, nv2)
        e.reduce_sum(kb.srow[0:16, 0:1], cf(SM, 4, 16), axis=AX.X).then_inc(v.h, 1)
        e.wait_ge(v.h, nv2 + 1)
        e.reciprocal(out=kb.srow[0:16, 0:1], in_=kb.srow[0:16, 0:1]).then_inc(v.h, 1)
        e.wait_ge(v.h, nv2 + 2)
        e.tensor_scalar(out=cf(SM, 4, 16), in0=cf(SM, 4, 16),
                        scalar1=kb.srow[0:16, 0:1], scalar2=None, op0=ALU.mult)
        e.memset(cf(SM + 16, 12, 16), 0.0)
        e.reduce_sum(kb.srow[0:4, 4:5], cf(MR, 2048, 4), axis=AX.X).then_inc(v.h, 1)
        e.wait_ge(v.h, nv2 + 3)
        e.tensor_scalar(out=kb.srow[0:4, 4:5], in0=kb.srow[0:4, 4:5],
                        scalar1=EPS_A, scalar2=None, op0=ALU.is_ge)
        e.tensor_scalar(out=kb.sAcol[:], in0=kb.sAcol[:], scalar1=-1.0, scalar2=None,
                        op0=ALU.mult).then_inc(v.h, 1)
    ctx.emit("vector", f_v2)
    v.bump(3)
    nv3 = v.bump(1)

    b0 = pool.acquire(ctx)
    b1 = pool.acquire(ctx)

    def f_t(e):
        e.wait_ge(v.h, nv3)
        e.wait_ge(g.h, ng)
        e.transpose(out=b0.f32(16, 16), in_=cf(SM, 16, 16), identity=kb.ident_f[0:16, 0:16])
        e.transpose(out=b1.f32(NH, NH), in_=cf(DP, NH, NH),
                    identity=kb.ident_f[0:NH, 0:NH]).then_inc(t.h, 1)
    ctx.emit("tensor", f_t)
    nt1 = t.bump(1)

    def f_v3(e):
        e.wait_ge(t.h, nt1)
        e.tensor_scalar(out=kb.wact[:], in0=b0.f32(4, 16),
                        scalar1=kb.srow[0:4, 4:5], scalar2=None,
                        op0=ALU.mult).then_inc(b0.done.h, 1)
        e.tensor_copy(out=kb.srow[0:1, 8:8 + NH], in_=b1.f32(1, NH)).then_inc(b1.done.h, 1)
        e.drain().then_inc(v.h, 1)
    ctx.emit("vector", f_v3)
    b0.done.bump(1)
    b1.done.bump(1)
    pool.release(b0)
    pool.release(b1)
    nv4 = v.bump(1)

    b2 = pool.acquire(ctx)

    def f_t2(e):
        e.wait_ge(v.h, nv4)
        e.matmul(out=b2.f32(128, NH), lhsT=kb.ones_f[0:1, :],
                 rhs=kb.srow[0:1, 8:8 + NH], start=True, stop=True).then_inc(t.h, 1)
    ctx.emit("tensor", f_t2)
    nt2 = t.bump(1)

    def f_v4(e):
        e.wait_ge(t.h, nt2)
        e.tensor_copy(out=kb.sDbc[:], in_=b2.f32(128, NH)).then_inc(b2.done.h, 1)
    ctx.emit("vector", f_v4)
    b2.done.bump(1)
    pool.release(b2)
    ctx.flush()


def _ln_chunk(ctx, kb, x_tile, xln_tile, x2scr, sems):
    pool = kb.pool
    sq, st, rw, bc, ap_ = sems["sq"], sems["st"], sems["rw"], sems["bc"], sems["ap"]
    bS = pool.acquire(ctx)
    bQ = pool.acquire(ctx)
    base_st = st.n
    base_ap = ap_.n
    for kc in range(NKC):
        def f_a(e, kc=kc, base_st=base_st, base_ap=base_ap):
            if base_st + kc > 0:
                e.wait_ge(st.h, base_st + kc)
            if kc == 0 and base_ap > 0:
                e.wait_ge(ap_.h, base_ap)
            e.activation(out=x2scr, in_=x_tile(kc), func=AF.Square).then_inc(sq.h, 1)
        ctx.emit("scalar", f_a)
        nsq = sq.bump(1)

        def f_t(e, kc=kc, nsq=nsq, bS=bS, bQ=bQ):
            e.wait_ge(sq.h, nsq)
            e.matmul(out=bS.f32(1, TC), lhsT=kb.ones_f[:, 0:1], rhs=x_tile(kc),
                     start=(kc == 0), stop=(kc == NKC - 1))
            e.matmul(out=bQ.f32(1, TC), lhsT=kb.ones_f[:, 0:1], rhs=x2scr,
                     start=(kc == 0), stop=(kc == NKC - 1)).then_inc(st.h, 1)
        ctx.emit("tensor", f_t)
        st.bump(1)
    nst = st.n
    mean_r = kb.srow[0:1, 0:TC]
    rstd_r = kb.srow[0:1, TC:2 * TC]
    tmp_r = kb.srow[0:1, 2 * TC:3 * TC]

    def f_v(e, bS=bS, bQ=bQ, nst=nst):
        e.wait_ge(st.h, nst)
        e.tensor_scalar(out=mean_r, in0=bS.f32(1, TC), scalar1=1.0 / HID, scalar2=None,
                        op0=ALU.mult).then_inc(bS.done.h, 1)
        e.tensor_scalar(out=rstd_r, in0=bQ.f32(1, TC), scalar1=1.0 / HID, scalar2=None,
                        op0=ALU.mult).then_inc(bQ.done.h, 1)
        e.drain()
        e.tensor_tensor(out=tmp_r, in0=mean_r, in1=mean_r, op=ALU.mult)
        e.drain()
        e.tensor_tensor(out=rstd_r, in0=rstd_r, in1=tmp_r, op=ALU.subtract).then_inc(rw.h, 1)
    ctx.emit("vector", f_v)
    bS.done.bump(1)
    bQ.done.bump(1)
    kb.pool.release(bS)
    kb.pool.release(bQ)
    nrw = rw.bump(1)

    def f_s(e, nrw=nrw):
        e.wait_ge(rw.h, nrw)
        e.activation(out=rstd_r, in_=rstd_r, func=AF.Sqrt, bias=kb.sconst[0:1, 0:1]).then_inc(rw.h, 1)
    ctx.emit("scalar", f_s)
    nrw2 = rw.bump(1)

    def f_v2(e, nrw2=nrw2):
        e.wait_ge(rw.h, nrw2)
        e.reciprocal(out=rstd_r, in_=rstd_r).then_inc(rw.h, 1)
    ctx.emit("vector", f_v2)
    nrw3 = rw.bump(1)

    bM = pool.acquire(ctx)
    bR = pool.acquire(ctx)

    def f_t2(e, nrw3=nrw3, bM=bM, bR=bR):
        e.wait_ge(rw.h, nrw3)
        e.matmul(out=bM.f32(128, TC), lhsT=kb.ones_f[0:1, :], rhs=mean_r,
                 start=True, stop=True)
        e.matmul(out=bR.f32(128, TC), lhsT=kb.ones_f[0:1, :], rhs=rstd_r,
                 start=True, stop=True).then_inc(bc.h, 1)
    ctx.emit("tensor", f_t2)
    nbc = bc.bump(1)
    for kc in range(NKC):
        def f_v3(e, kc=kc, bM=bM, bR=bR, nbc=nbc):
            if kc == 0:
                e.wait_ge(bc.h, nbc)
            e.drain()
            e.tensor_tensor(out=x2scr, in0=x_tile(kc), in1=bM.f32(128, TC), op=ALU.subtract)
            e.drain()
            m = e.tensor_tensor(out=xln_tile(kc), in0=x2scr, in1=bR.f32(128, TC), op=ALU.mult)
            m.then_inc(ap_.h, 1)
            if kc == NKC - 1:
                e.drain().then_inc(bM.done.h, 1)
                e.drain().then_inc(bR.done.h, 1)
        ctx.emit("vector", f_v3)
        ap_.bump(1)
    bM.done.bump(1)
    bR.done.bump(1)
    kb.pool.release(bM)
    kb.pool.release(bR)
    return ap_.n


def _phase_A1(ctx, kb):
    cf, cb = kb.cf, kb.cb
    pool = kb.pool
    X, SCR = 0, 32768
    WQ, MEM, MN = 34816, 100352, 116736
    XLN, SIG, SS_, CMB = 116864, 133248, 149632, 151680
    T1, T2, F4, SEL4, F4B = 168064, 170112, 172160, 174208, 176256

    S = ctx.sem_cached
    dW = ctx.sem(); dX = S("LNdX"); sq = S("LNsq"); st = S("LNst")
    rw = S("LNrw"); bc = S("LNbc"); ap_ = S("LNap"); qs = ctx.sem()
    sg = ctx.sem(); fr = ctx.sem(); cm = ctx.sem(); so = ctx.sem()

    gsel = ctx.sem()

    def f_g(e):
        for n in range(NBANK):
            reg = cf(SEL4 + n * 512, 128, 4)
            e.memset(reg, 1.0)
            e.drain()
            e.affine_select(out=reg, in_=reg, compare_op=ALU.is_ge, fill=0.0,
                            base=-n, pattern=[[0, 128]], channel_multiplier=1)
            e.drain()
            e.affine_select(out=reg, in_=reg, compare_op=ALU.is_ge, fill=0.0,
                            base=n, pattern=[[0, 128]], channel_multiplier=-1)
        e.drain().then_inc(gsel.h, 1)
        for kc in range(NKC):
            e.dma_start(out=cb(WQ + kc * 4096, 2048),
                        in_=kb.d_wqT[kc * 128:(kc + 1) * 128, :]).then_inc(dW.h, 16)
        e.dma_start(out=cb(MEM, 8192), in_=kb.d_memT).then_inc(dW.h, 16)
        e.dma_start(out=cb(MN, 64), in_=kb.d_mnT).then_inc(dW.h, 16)
    ctx.emit("gpsimd", f_g)
    nW = dW.bump(16 * (NKC + 2))

    for c in range(NTC):
        nso_prev = so.n

        def f_sy(e, c=c):
            if c > 0:
                e.wait_ge(ap_.h, c * NKC)
            e.dma_start(out=sb3(cf(X, 8192)), in_=chunk_ap(kb.d_xt, c)).then_inc(dX.h, 16)
        ctx.emit("sync", f_sy)
        nX = dX.bump(16)

        for eng in ("tensor", "scalar", "vector"):
            def f_w(e, nX=nX, c=c, eng=eng):
                e.wait_ge(dX.h, nX)
                if c == 0 and eng == "tensor":
                    e.wait_ge(dW.h, nW)
            ctx.emit(eng, f_w)

        x_tile = lambda kc: cf(X + kc * 2048, TC)
        xln_tile = lambda kc: cb(XLN + kc * 1024, TC)
        nln = _ln_chunk(ctx, kb, x_tile, xln_tile, cf(SCR, TC),
                        dict(sq=sq, st=st, rw=rw, bc=bc, ap=ap_))
        if kb.dbg and c == 0:
            dbs = ctx.sem()

            def f_dbg(e, nln=nln):
                e.wait_ge(ap_.h, nln)
                e.dma_start(out=chunk_ap(kb.d_dbgb, 0), in_=sb3(cb(XLN, 8192))).then_inc(dbs.h, 16)
                e.dma_start(out=kb.d_dbgr, in_=kb.srow[:]).then_inc(dbs.h, 16)
            ctx.emit("sync", f_dbg)
            ctx.pending.append((dbs, dbs.bump(32)))

        for j in range(NKC):
            bq = pool.acquire(ctx)

            def f_t(e, j=j, bq=bq, nln=nln):
                if j == 0:
                    e.wait_ge(ap_.h, nln)
                for kc in range(NKC):
                    m = e.matmul(out=bq.f32(128, TC), lhsT=cb(WQ + kc * 4096 + j * 256, 128),
                                 rhs=xln_tile(kc), start=(kc == 0), stop=(kc == NKC - 1))
                m.then_inc(qs.h, 1)
            ctx.emit("tensor", f_t)
            nq = qs.bump(1)

            def f_v(e, bq=bq, nq=nq):
                e.wait_ge(qs.h, nq)
                e.tensor_scalar(out=cf(T1, 512), in0=bq.f32(128, TC), scalar1=0.0,
                                scalar2=None, op0=ALU.min).then_inc(sg.h, 1)
            ctx.emit("vector", f_v)
            n1 = sg.bump(1)

            def f_s(e, n1=n1):
                e.wait_ge(sg.h, n1)
                e.activation(out=cf(T2, 512), in_=cf(T1, 512), func=AF.Exp).then_inc(sg.h, 1)
            ctx.emit("scalar", f_s)
            n2 = sg.bump(1)

            def f_v2(e, j=j, bq=bq, n2=n2):
                e.wait_ge(sg.h, n2)
                e.scalar_tensor_tensor(out=cb(SIG + j * 1024, TC), in0=bq.f32(128, TC),
                                       scalar=0.0, in1=cf(T2, 512), op0=ALU.max,
                                       op1=ALU.add).then_inc(bq.done.h, 1)
                e.drain().then_inc(sg.h, 1)
            ctx.emit("vector", f_v2)
            bq.done.bump(1)
            pool.release(bq)
            sg.bump(1)
        nsig = sg.n
        if kb.dbg and c == 0:
            dbs2 = ctx.sem()

            def f_dbg2(e, nsig=nsig):
                e.wait_ge(sg.h, nsig)
                e.dma_start(out=chunk_ap(kb.d_dbgb, 1), in_=sb3(cb(SIG, 8192))).then_inc(dbs2.h, 16)
                e.dma_start(out=kb.d_dbgs[0:4, 0:512], in_=cf(SEL4, 512, 4)).then_inc(dbs2.h, 16)
                e.dma_start(out=kb.d_dbgs[4:8, 0:16], in_=kb.wact[:]).then_inc(dbs2.h, 16)
            ctx.emit("sync", f_dbg2)
            ctx.pending.append((dbs2, dbs2.bump(48)))

        for h in range(NHEAD_A):
            F4h = F4 if h % 2 == 0 else F4B
            sig_h = cb(SIG + h * 1024, TC)
            bn = pool.acquire(ctx)

            def f_t2(e, h=h, bn=bn, nsig=nsig, sig_h=sig_h):
                if h == 0:
                    e.wait_ge(sg.h, nsig)
                e.matmul(out=bn.f32(4, TC), lhsT=cb(MN + h * 8, 4), rhs=sig_h,
                         start=True, stop=True).then_inc(qs.h, 1)
            ctx.emit("tensor", f_t2)
            nn = qs.bump(1)

            def f_v3(e, h=h, bn=bn, nn=nn, nf0=fr.n, F4h=F4h):
                e.wait_ge(qs.h, nn)
                e.tensor_scalar(out=cf(F4h, 512, 4), in0=bn.f32(4, TC), scalar1=EPS_A,
                                scalar2=None, op0=ALU.max).then_inc(bn.done.h, 1)
                e.drain()
                e.reciprocal(out=cf(F4h, 512, 4), in_=cf(F4h, 512, 4)).then_inc(fr.h, 1)
                e.wait_ge(fr.h, nf0 + 1)
                e.tensor_scalar(out=cf(F4h, 512, 4), in0=cf(F4h, 512, 4),
                                scalar1=kb.wact[:, h:h + 1], scalar2=None,
                                op0=ALU.mult).then_inc(fr.h, 1)
            ctx.emit("vector", f_v3)
            bn.done.bump(1)
            pool.release(bn)
            fr.bump(1)
            nf = fr.bump(1)
            if kb.dbg and c == 0 and h == 0:
                dbs3 = ctx.sem()

                def f_dbg3(e, nf=nf):
                    e.wait_ge(fr.h, nf)
                    e.dma_start(out=kb.d_dbgs[8:12, 0:512], in_=cf(F4, 512, 4)).then_inc(dbs3.h, 16)
                ctx.emit("sync", f_dbg3)
                ctx.pending.append((dbs3, dbs3.bump(16)))
                ctx.barrier()

            bC = pool.acquire(ctx)
            for n in range(NBANK):
                bb = pool.acquire(ctx)

                def f_t3(e, n=n, bb=bb, nf=nf, F4h=F4h):
                    if n == 0:
                        e.wait_ge(fr.h, nf)
                        e.wait_ge(gsel.h, 1)
                    e.matmul(out=bb.f32(128, TC), lhsT=cf(SEL4 + n * 512, 128, 4),
                             rhs=cf(F4h, 512, 4), start=True,
                             stop=True).then_inc(qs.h, 1)
                ctx.emit("tensor", f_t3)
                nb = qs.bump(1)

                def f_v4(e, n=n, h=h, bb=bb, nb=nb, sig_h=sig_h):
                    e.wait_ge(qs.h, nb)
                    e.tensor_tensor(out=cb(SS_ + (n % 2) * 1024, TC), in0=sig_h,
                                    in1=bb.f32(128, TC), op=ALU.mult).then_inc(sg.h, 1)
                    e.drain().then_inc(bb.done.h, 1)
                ctx.emit("vector", f_v4)
                bb.done.bump(1)
                pool.release(bb)
                nss = sg.bump(1)

                def f_t4(e, n=n, h=h, bC=bC, nss=nss):
                    e.wait_ge(sg.h, nss)
                    m = e.matmul(out=bC.f32(128, TC),
                                 lhsT=cb(MEM + (n * 16 + h) * 256, 128),
                                 rhs=cb(SS_ + (n % 2) * 1024, TC),
                                 start=(n == 0), stop=(n == NBANK - 1))
                    m.then_inc(qs.h, 1)
                ctx.emit("tensor", f_t4)
                qs.bump(1)
            nq4 = qs.n

            def f_v5(e, h=h, c=c, bC=bC, nq4=nq4, nso_prev=nso_prev):
                e.wait_ge(qs.h, nq4)
                if h == 0 and nso_prev > 0:
                    e.wait_ge(so.h, nso_prev)
                e.tensor_copy(out=cb(CMB + h * 1024, TC),
                              in_=bC.f32(128, TC)).then_inc(bC.done.h, 1)
                e.drain().then_inc(cm.h, 1)
            ctx.emit("vector", f_v5)
            bC.done.bump(1)
            pool.release(bC)
            cm.bump(1)
        ncall = cm.n

        def f_sy2(e, c=c, ncall=ncall):
            e.wait_ge(cm.h, ncall)
            e.dma_start(out=chunk_ap(kb.d_comb, c), in_=sb3(cb(CMB, 8192))).then_inc(so.h, 16)
        ctx.emit("sync", f_sy2)
        so.bump(16)

    ctx.pending.append((so, so.n))
    ctx.flush()


def _phase_A2(ctx, kb):
    cf, cb = kb.cf, kb.cb
    pool = kb.pool
    X, WO, CMB, X1 = 0, 34816, 100352, 118784

    dW = ctx.sem(); dX = ctx.sem(); qs = ctx.sem(); ev = ctx.sem(); so = ctx.sem()

    def f_g(e):
        for kc in range(NKC):
            e.dma_start(out=cb(WO + kc * 4096, 2048),
                        in_=kb.d_woT[kc * 128:(kc + 1) * 128, :]).then_inc(dW.h, 16)
    ctx.emit("gpsimd", f_g)
    nW = dW.bump(16 * NKC)

    for c in range(NTC):
        nso_prev = so.n

        def f_sy(e, c=c):
            if c > 0:
                e.wait_ge(ev.h, c * NKC)
            e.dma_start(out=sb3(cf(X, 8192)), in_=chunk_ap(kb.d_xt, c)).then_inc(dX.h, 16)
            e.dma_start(out=sb3(cb(CMB, 8192)), in_=chunk_ap(kb.d_comb, c)).then_inc(dX.h, 16)
        ctx.emit("sync", f_sy)
        nX = dX.bump(32)

        for j in range(NKC):
            bq = pool.acquire(ctx)

            def f_t(e, j=j, c=c, bq=bq, nX=nX):
                if j == 0:
                    e.wait_ge(dX.h, nX)
                    if c == 0:
                        e.wait_ge(dW.h, nW)
                for kc in range(NKC):
                    m = e.matmul(out=bq.f32(128, TC), lhsT=cb(WO + kc * 4096 + j * 256, 128),
                                 rhs=cb(CMB + kc * 1024, TC), start=(kc == 0),
                                 stop=(kc == NKC - 1))
                m.then_inc(qs.h, 1)
            ctx.emit("tensor", f_t)
            nq = qs.bump(1)

            def f_v(e, j=j, bq=bq, nq=nq, nso_prev=nso_prev):
                e.wait_ge(qs.h, nq)
                if j == 0 and nso_prev > 0:
                    e.wait_ge(so.h, nso_prev)
                e.tensor_tensor(out=cf(X1 + j * 2048, TC), in0=cf(X + j * 2048, TC),
                                in1=bq.f32(128, TC), op=ALU.add).then_inc(bq.done.h, 1)
                e.drain().then_inc(ev.h, 1)
            ctx.emit("vector", f_v)
            bq.done.bump(1)
            pool.release(bq)
            ev.bump(1)
        nev = ev.n

        def f_sy2(e, c=c, nev=nev):
            e.wait_ge(ev.h, nev)
            e.dma_start(out=chunk_ap(kb.d_x1, c), in_=sb3(cf(X1, 8192))).then_inc(so.h, 16)
        ctx.emit("sync", f_sy2)
        so.bump(16)

    ctx.pending.append((so, so.n))
    ctx.flush()


WN = 256
NWIN = MLP // WN


def _dequant_tile(ctx, kb, eng, pk_i32, scale_psum, out_lo, out_hi, wid, dqo):
    ci = kb.ci
    DQ1, DQ2, DQB = dqo

    def f(e, pk=pk_i32, sp=scale_psum, out_lo=out_lo, out_hi=out_hi):
        for (shift, out) in ((0, out_lo), (4, out_hi)):
            if shift == 0:
                e.tensor_scalar(out=ci(DQ1, wid), in0=pk, scalar1=7, scalar2=None,
                                op0=ALU.bitwise_and)
                e.tensor_scalar(out=ci(DQ2, wid), in0=pk, scalar1=8, scalar2=28,
                                op0=ALU.bitwise_and, op1=ALU.logical_shift_left)
            else:
                e.tensor_scalar(out=ci(DQ1, wid), in0=pk, scalar1=4, scalar2=7,
                                op0=ALU.logical_shift_right, op1=ALU.bitwise_and)
                e.tensor_scalar(out=ci(DQ2, wid), in0=pk, scalar1=7, scalar2=31,
                                op0=ALU.logical_shift_right, op1=ALU.logical_shift_left)
            e.drain()
            e.tensor_scalar(out=ci(DQB, wid), in0=ci(DQ1, wid), scalar1=252,
                            scalar2=None, op0=ALU.mult)
            e.drain()
            e.scalar_tensor_tensor(out=ci(DQB, wid), in0=ci(DQ1, wid), scalar=252,
                                   in1=ci(DQB, wid), op0=ALU.add, op1=ALU.min)
            e.drain()
            e.scalar_tensor_tensor(out=ci(DQB, wid), in0=ci(DQB, wid),
                                   scalar=4194304, in1=ci(DQ2, wid),
                                   op0=ALU.mult, op1=ALU.add)
            e.drain()
            e.tensor_tensor(out=out, in0=ci(DQB, wid).bitcast(F32), in1=sp,
                            op=ALU.mult)
            e.drain()
    ctx.emit(eng, f)


def _phase_Fa(ctx, kb, call):
    cf, cb, ci = kb.cf, kb.cb, kb.ci
    pool = kb.pool
    xsrc = kb.d_x1 if call == 1 else kb.d_x3

    XLN = 0
    WG = 131072
    WU = 147456
    PK = 163840
    DQ1, DQ2, DQB = 165888, 166912, 167936
    INTER = 168960
    GS = 177152
    XB = 131072
    SCRB = 167936

    S = ctx.sem_cached
    dX = S("LNdX"); sq = S("LNsq"); st = S("LNst"); rw = S("LNrw")
    bc = S("LNbc"); ap_ = S("LNap"); dqv = S("Fadqv")
    mmv = S("Fammv"); mmw = S("Fammw"); iv = S("Faiv")
    so = S("Faso"); dsc = S("Fadsc")
    dqs = [S("Fadqs0"), S("Fadqs1")]
    css = [S("Facss0"), S("Facss1")]
    dqw = [S("Fadqw0"), S("Fadqw1")]

    base_ap = ap_.n
    for c in range(NTC):
        def f_sy(e, c=c, base_ap=base_ap):
            if c > 0:
                e.wait_ge(ap_.h, base_ap + c * NKC)
            e.dma_start(out=sb3(cf(XB, 8192)), in_=chunk_ap(xsrc, c)).then_inc(dX.h, 16)
        ctx.emit("sync", f_sy)
        nX = dX.bump(16)
        for eng in ("tensor", "scalar", "vector"):
            ctx.emit(eng, lambda e, nX=nX: e.wait_ge(dX.h, nX))
        x_tile = lambda kc, c=c: cf(XB + kc * 2048, TC)
        xln_tile = lambda kc, c=c: cb(XLN + kc * 8192 + c * 1024, TC)
        _ln_chunk(ctx, kb, x_tile, xln_tile, cf(SCRB, TC),
                  dict(sq=sq, st=st, rw=rw, bc=bc, ap=ap_))
    ctx.pending.append((ap_, ap_.n))
    ctx.barrier()

    wg_view = cb(WG, 2 * 16 * WN).rearrange("p (b k n) -> p b k n", b=2, n=WN)
    wu_view = cb(WU, 2 * 16 * WN).rearrange("p (b k n) -> p b k n", b=2, n=WN)
    last_dsc = [dsc.n, dsc.n]
    base_mmw = mmw.n
    base_mmv = mmv.n
    SCX = 181248

    def emit_strokes(w):
        buf = w % 2
        if call == 1:
            for mi in range(2):
                dpk = kb.d_gpT if mi == 0 else kb.d_upT
                dsct = kb.d_gsT if mi == 0 else kb.d_usT
                wv = wg_view if mi == 0 else wu_view
                for cc in range(8):
                    slot = (w * 16 + mi * 8 + cc) % 2
                    dqsem = dqs[slot]

                    def f_ld(e, w=w, cc=cc, dpk=dpk, dsct=dsct, mi=mi, slot=slot,
                             scgate=last_dsc[mi], pkgate=dqv.n - 1, dqsem=dqsem):
                        if pkgate > 0:
                            e.wait_ge(dqv.h, pkgate)
                        if scgate > 0:
                            e.wait_ge(dsc.h, scgate)
                        e.dma_start(out=ci(PK + slot * 1024, WN),
                                    in_=dpk[cc * 128:(cc + 1) * 128,
                                            w * WN:(w + 1) * WN]).then_inc(dqsem.h, 16)
                        e.dma_start(out=kb.srow[0:8, mi * 512:mi * 512 + WN],
                                    in_=dsct[8 * cc:8 * cc + 8,
                                             w * WN:(w + 1) * WN]).then_inc(dqsem.h, 16)
                    ctx.emit("sync", f_ld)
                    ndq = dqsem.bump(32)

                    psc = pool.acquire(ctx)

                    def f_sc(e, psc=psc, ndq=ndq, mi=mi, dqsem=dqsem):
                        e.wait_ge(dqsem.h, ndq)
                        e.matmul(out=psc.f32(128, WN), lhsT=kb.epat[:],
                                 rhs=kb.srow[0:8, mi * 512:mi * 512 + WN],
                                 start=True, stop=True).then_inc(dsc.h, 1)
                    ctx.emit("tensor", f_sc)
                    last_dsc[mi] = dsc.bump(1)

                    def f_se(e, psc=psc, slot=slot, ndsc=last_dsc[mi],
                             xgate=dqv.n - 1):
                        e.wait_ge(dsc.h, ndsc)
                        if xgate > 0:
                            e.wait_ge(dqv.h, xgate)
                        e.activation(out=cf(SCX + slot * 1024, WN),
                                     in_=psc.f32(128, WN),
                                     func=AF.Identity).then_inc(psc.done.h, 1)
                    ctx.emit("scalar", f_se)
                    psc.done.bump(1)
                    pool.release(psc)
                    nse = ctx.sem_cached("Fase")
                    ctx.emit("scalar", lambda e, s=nse: e.drain().then_inc(s.h, 1))
                    nsev = nse.bump(1)

                    def f_gate(e, nsev=nsev, w=w, csgate=css[buf].n,
                               bw=base_mmw, bv=base_mmv, s=nse):
                        e.wait_ge(s.h, nsev)
                        if w >= 2:
                            e.wait_ge(mmw.h, bw + (w - 1) * 16)
                            e.wait_ge(mmv.h, bv + (w - 1) * 16)
                            e.wait_ge(css[w % 2].h, csgate)
                    ctx.emit("vector", f_gate)
                    _dequant_tile(ctx, kb, "vector", ci(PK + slot * 1024, WN),
                                  cf(SCX + slot * 1024, WN), wv[:, buf, 2 * cc, :],
                                  wv[:, buf, 2 * cc + 1, :], WN, (DQ1, DQ2, DQB))

                    def f_dqd(e):
                        e.drain().then_inc(dqv.h, 1)
                    ctx.emit("vector", f_dqd)
                    dqv.bump(1)

            def f_cs(e, w=w, buf=buf, nv=dqv.n):
                e.wait_ge(dqv.h, nv)
                e.dma_start(out=kb.d_wgC[w].rearrange("(a p) n -> p a n", p=128),
                            in_=cb(WG + buf * 8192, 16 * WN)
                            .rearrange("p (a n) -> p a n", n=WN)).then_inc(css[buf].h, 16)
                e.dma_start(out=kb.d_wuC[w].rearrange("(a p) n -> p a n", p=128),
                            in_=cb(WU + buf * 8192, 16 * WN)
                            .rearrange("p (a n) -> p a n", n=WN)).then_inc(css[buf].h, 16)
            ctx.emit("scalar", f_cs)
            css[buf].bump(32)
        else:
            def f_ldc(e, w=w, buf=buf, bw=base_mmw, bv=base_mmv):
                if w >= 2:
                    e.wait_ge(mmw.h, bw + (w - 1) * 16)
                    e.wait_ge(mmv.h, bv + (w - 1) * 16)
                e.dma_start(out=cb(WG + buf * 8192, 16 * WN)
                            .rearrange("p (a n) -> p a n", n=WN),
                            in_=kb.d_wgC[w].rearrange("(a p) n -> p a n", p=128)
                            ).then_inc(dqw[buf].h, 16)
                e.dma_start(out=cb(WU + buf * 8192, 16 * WN)
                            .rearrange("p (a n) -> p a n", n=WN),
                            in_=kb.d_wuC[w].rearrange("(a p) n -> p a n", p=128)
                            ).then_inc(dqw[buf].h, 16)
            ctx.emit("sync", f_ldc)
            dqw[buf].bump(32)

    rdy_marks = {}

    def emit_mms(w):
        buf = w % 2
        ndq_need = rdy_marks[w]
        rdy = dqv if call == 1 else dqw[buf]
        for nj in range(2):
            for tc in range(NTC):
                gslot = (w * 16 + nj * 8 + tc) % 2
                pg = pool.acquire(ctx)
                pu = pool.acquire(ctx)

                def f_mm(e, buf=buf, nj=nj, tc=tc, pg=pg, pu=pu,
                         ndq_need=ndq_need, rdy=rdy, first=(nj == 0 and tc == 0)):
                    if first:
                        e.wait_ge(rdy.h, ndq_need)
                    for kbk in range(NKC):
                        m = e.matmul(out=pg.f32(128, TC),
                                     lhsT=cb(WG + buf * 8192 + kbk * 512 + nj * 256, 128),
                                     rhs=cb(XLN + kbk * 8192 + tc * 1024, TC),
                                     start=(kbk == 0), stop=(kbk == NKC - 1))
                    m.then_inc(mmv.h, 1)
                    for kbk in range(NKC):
                        m = e.matmul(out=pu.f32(128, TC),
                                     lhsT=cb(WU + buf * 8192 + kbk * 512 + nj * 256, 128),
                                     rhs=cb(XLN + kbk * 8192 + tc * 1024, TC),
                                     start=(kbk == 0), stop=(kbk == NKC - 1))
                    m.then_inc(mmw.h, 1)
                ctx.emit("tensor", f_mm)
                nv = mmv.bump(1)
                nw = mmw.bump(1)

                def f_gelu(e, pg=pg, nv=nv, gslot=gslot, ivgate=iv.n - 1):
                    e.wait_ge(mmv.h, nv)
                    if ivgate > 0:
                        e.wait_ge(iv.h, ivgate)
                    e.activation(out=cf(GS + gslot * 2048, 512), in_=pg.f32(128, TC),
                                 func=AF.Gelu_apprx_tanh).then_inc(pg.done.h, 1)
                ctx.emit("scalar", f_gelu)
                npgd = pg.done.bump(1)
                pool.release(pg)

                def f_int(e, pu=pu, nw=nw, npgd=npgd, pg=pg, tc=tc, nso=so.n,
                          gslot=gslot):
                    e.wait_ge(mmw.h, nw)
                    e.wait_ge(pg.done.h, npgd)
                    if nso > 0:
                        e.wait_ge(so.h, nso)
                    e.tensor_tensor(out=cb(INTER + tc * 1024, TC),
                                    in0=cf(GS + gslot * 2048, 512),
                                    in1=pu.f32(128, TC), op=ALU.mult).then_inc(pu.done.h, 1)
                    e.drain().then_inc(iv.h, 1)
                ctx.emit("vector", f_int)
                pu.done.bump(1)
                pool.release(pu)
                iv.bump(1)

            def f_si(e, w=w, nj=nj, niv=iv.n):
                e.wait_ge(iv.h, niv)
                rb = w * 2 + nj
                e.dma_start(out=kb.d_inter[rb * 128:(rb + 1) * 128, :],
                            in_=cb(INTER, 4096)).then_inc(so.h, 16)
            ctx.emit("sync", f_si)
            so.bump(16)

    emit_strokes(0)
    rdy_marks[0] = dqv.n if call == 1 else dqw[0].n
    emit_strokes(1)
    rdy_marks[1] = dqv.n if call == 1 else dqw[1].n
    for w in range(NWIN):
        emit_mms(w)
        if w + 2 < NWIN:
            emit_strokes(w + 2)
            rdy_marks[w + 2] = dqv.n if call == 1 else dqw[w % 2].n
    ctx.pending.append((so, so.n))
    if call == 1:
        ctx.pending.append((css[0], css[0].n))
        ctx.pending.append((css[1], css[1].n))
    ctx.flush()


def _phase_Fb(ctx, kb, call):
    cf, cb, ci = kb.cf, kb.cb, kb.ci
    pool = kb.pool
    x_in = kb.d_x1 if call == 1 else kb.d_x3
    x_out = kb.d_x2 if call == 1 else kb.d_out

    INTERS = 0
    WD = 131072
    PKD = 163840
    DQ1, DQ2, DQB = 167936, 169984, 172032
    X1B = 174080
    X2B = 178176

    S = ctx.sem_cached
    dI = S("FbdI"); dsc = S("Fbdsc"); dqv = S("Fbdqv")
    mmu = S("Fbmmu"); ev = S("Fbev")
    dqs = [S("Fbdqs0"), S("Fbdqs1")]
    css = [S("Fbcss0"), S("Fbcss1")]
    dws = [S("Fbdws0"), S("Fbdws1"), S("Fbdws2"), S("Fbdws3")]
    dx1s = [S("Fbdx1s0"), S("Fbdx1s1")]
    sos = [S("Fbsos0"), S("Fbsos1")]

    for tc in range(NTC):
        def f_li(e, tc=tc, nmu=mmu.n):
            if nmu > 0:
                e.wait_ge(mmu.h, nmu)
            for part in range(8):
                e.dma_start(out=sb3(cb(INTERS + part * 16 * 1024, 16 * TC)),
                            in_=kb.d_inter.rearrange("k (c t) -> k c t", t=TC)
                            [part * 2048:(part + 1) * 2048, tc, :]
                            .rearrange("(a p) t -> p a t", p=128)).then_inc(dI.h, 16)
        ctx.emit("sync", f_li)
        ndI = dI.bump(16 * 8)

        for hw in range(4):
            acc = [pool.acquire(ctx) for _ in range(4)]

            def f_tw(e, ndI=ndI, first=(hw == 0)):
                if first:
                    e.wait_ge(dI.h, ndI)
            ctx.emit("tensor", f_tw)

            if call == 1 and tc == 0:
                base_mmu = mmu.n
                base_dqv = dqv.n
                base_dsc = dsc.n
                for npc in range(64):
                    slot = npc % 2

                    dqsem = dqs[slot]

                    def f_lp(e, npc=npc, hw=hw, slot=slot,
                             pkgate=(base_dqv + npc - 1), dqsem=dqsem,
                             scgate=(base_dsc + npc)):
                        if pkgate > 0:
                            e.wait_ge(dqv.h, pkgate)
                        if scgate > 0:
                            e.wait_ge(dsc.h, scgate)
                        e.dma_start(out=ci(PKD + slot * 2048, 512),
                                    in_=kb.d_dpT[npc * 128:(npc + 1) * 128,
                                                 hw * 512:(hw + 1) * 512]).then_inc(dqsem.h, 16)
                        e.dma_start(out=kb.srow[0:8, 0:512],
                                    in_=kb.d_dsT[8 * npc:8 * npc + 8,
                                                 hw * 512:(hw + 1) * 512]).then_inc(dqsem.h, 16)
                    ctx.emit("sync", f_lp)
                    ndq = dqsem.bump(32)
                    psc = pool.acquire(ctx)

                    def f_sc(e, psc=psc, ndq=ndq, dqsem=dqsem):
                        e.wait_ge(dqsem.h, ndq)
                        e.matmul(out=psc.f32(128, TC), lhsT=kb.epat[:],
                                 rhs=kb.srow[0:8, 0:512], start=True,
                                 stop=True).then_inc(dsc.h, 1)
                    ctx.emit("tensor", f_sc)
                    ndsc = dsc.bump(1)

                    def f_gate(e, ndsc=ndsc, npc=npc,
                               mgate=(base_mmu + npc - 1),
                               cgate=css[slot].n):
                        e.wait_ge(dsc.h, ndsc)
                        if mgate > 0:
                            e.wait_ge(mmu.h, mgate)
                        if cgate > 0:
                            e.wait_ge(css[npc % 2].h, cgate)
                    ctx.emit("vector", f_gate)
                    _dequant_tile(ctx, kb, "vector", ci(PKD + slot * 2048, 512),
                                  psc.f32(128, TC),
                                  cb(WD + (slot * 2 + 0) * 1024, 512),
                                  cb(WD + (slot * 2 + 1) * 1024, 512), 512,
                                  (DQ1, DQ2, DQB))

                    def f_dqd(e, psc=psc):
                        e.drain().then_inc(psc.done.h, 1)
                        e.drain().then_inc(dqv.h, 1)
                    ctx.emit("vector", f_dqd)
                    psc.done.bump(1)
                    pool.release(psc)
                    ndqv = dqv.bump(1)

                    def f_mm(e, npc=npc, slot=slot, acc=tuple(acc), ndqv=ndqv):
                        e.wait_ge(dqv.h, ndqv)
                        for nib in range(2):
                            for hj in range(4):
                                m = e.matmul(out=acc[hj].f32(128, TC),
                                             lhsT=cb(WD + (slot * 2 + nib) * 1024 + hj * 256, 128),
                                             rhs=cb(INTERS + (2 * npc + nib) * 1024, TC),
                                             start=(npc == 0 and nib == 0),
                                             stop=(npc == 63 and nib == 1))
                        m.then_inc(mmu.h, 1)
                    ctx.emit("tensor", f_mm)
                    mmu.bump(1)

                    def f_cst(e, npc=npc, slot=slot, hw=hw, ndqv=ndqv):
                        e.wait_ge(dqv.h, ndqv)
                        e.dma_start(out=kb.d_wdC[hw][2 * npc * 128:(2 * npc + 2) * 128, :]
                                    .rearrange("(a p) n -> p a n", p=128),
                                    in_=cb(WD + slot * 2048, 2 * 512)
                                    .rearrange("p (a n) -> p a n", n=512)).then_inc(css[slot].h, 16)
                    ctx.emit("scalar", f_cst)
                    css[slot].bump(16)
            else:
                base_mmu = mmu.n
                for grp in range(16):
                    slot = grp % 4

                    def f_lw(e, grp=grp, hw=hw, slot=slot,
                             gate=(base_mmu + grp - 3),
                             csgate0=css[0].n, csgate1=css[1].n):
                        if gate > 0:
                            e.wait_ge(mmu.h, gate)
                        if csgate0 > 0:
                            e.wait_ge(css[0].h, csgate0)
                        if csgate1 > 0:
                            e.wait_ge(css[1].h, csgate1)
                        e.dma_start(out=cb(WD + slot * 8192, 8 * 512)
                                    .rearrange("p (a n) -> p a n", n=512),
                                    in_=kb.d_wdC[hw][grp * 1024:(grp + 1) * 1024, :]
                                    .rearrange("(a p) n -> p a n", p=128)).then_inc(dws[slot].h, 16)
                    ctx.emit("sync", f_lw)
                    ndw = dws[slot].bump(16)

                    def f_mm(e, grp=grp, slot=slot, acc=tuple(acc), ndw=ndw):
                        e.wait_ge(dws[slot].h, ndw)
                        for b8 in range(8):
                            ncb = grp * 8 + b8
                            for hj in range(4):
                                m = e.matmul(out=acc[hj].f32(128, TC),
                                             lhsT=cb(WD + slot * 8192 + b8 * 1024 + hj * 256, 128),
                                             rhs=cb(INTERS + ncb * 1024, TC),
                                             start=(ncb == 0), stop=(ncb == 127))
                        m.then_inc(mmu.h, 1)
                    ctx.emit("tensor", f_mm)
                    mmu.bump(1)

            for hj in range(4):
                slot = hj % 2
                hr = hw * 4 + hj

                def f_lx(e, hr=hr, tc=tc, slot=slot, nev=ev.n - 1):
                    if nev > 0:
                        e.wait_ge(ev.h, nev)
                    e.dma_start(out=cf(X1B + slot * 2048, 512),
                                in_=x_in.rearrange("k (c t) -> k c t", t=TC)
                                [hr * 128:(hr + 1) * 128, tc, :]).then_inc(dx1s[slot].h, 16)
                ctx.emit("sync", f_lx)
                ndx = dx1s[slot].bump(16)
                nmu_need = mmu.n

                def f_ev(e, hj=hj, slot=slot, acc=tuple(acc), ndx=ndx,
                         nmu_need=nmu_need, sgate=sos[slot].n):
                    e.wait_ge(mmu.h, nmu_need)
                    e.wait_ge(dx1s[slot].h, ndx)
                    if sgate > 0:
                        e.wait_ge(sos[slot].h, sgate)
                    e.tensor_tensor(out=cf(X2B + slot * 2048, 512),
                                    in0=cf(X1B + slot * 2048, 512),
                                    in1=acc[hj].f32(128, TC),
                                    op=ALU.add).then_inc(acc[hj].done.h, 1)
                    e.drain().then_inc(ev.h, 1)
                ctx.emit("vector", f_ev)
                acc[hj].done.bump(1)
                pool.release(acc[hj])
                ev.bump(1)

                def f_sx(e, hr=hr, tc=tc, slot=slot, nev=ev.n):
                    e.wait_ge(ev.h, nev)
                    e.dma_start(out=x_out.rearrange("k (c t) -> k c t", t=TC)
                                [hr * 128:(hr + 1) * 128, tc, :],
                                in_=cf(X2B + slot * 2048, 512)).then_inc(sos[slot].h, 16)
                ctx.emit("sync", f_sx)
                sos[slot].bump(16)

    ctx.pending.append((sos[0], sos[0].n))
    ctx.pending.append((sos[1], sos[1].n))
    if call == 1:
        ctx.pending.append((css[0], css[0].n))
        ctx.pending.append((css[1], css[1].n))
    ctx.flush()


def _phase_B1(ctx, kb):
    cf, cb = kb.cf, kb.cb
    pool = kb.pool

    XLN = 0
    XB = 131072
    SCRB = 163840
    WB = 131072
    PS = 139264
    PSD = 147456

    S = ctx.sem_cached
    dX = S("LNdX"); sq = S("LNsq"); st = S("LNst"); rw = S("LNrw")
    bc = S("LNbc"); ap_ = S("LNap"); mm = ctx.sem(); evs = ctx.sem()
    so = ctx.sem()
    dws = [ctx.sem(), ctx.sem()]

    base_ap = ap_.n
    for c in range(NTC):
        def f_sy(e, c=c, base_ap=base_ap):
            if c > 0:
                e.wait_ge(ap_.h, base_ap + c * NKC)
            e.dma_start(out=sb3(cf(XB, 8192)), in_=chunk_ap(kb.d_x2, c)).then_inc(dX.h, 16)
        ctx.emit("sync", f_sy)
        nX = dX.bump(16)
        for eng in ("tensor", "scalar", "vector"):
            ctx.emit(eng, lambda e, nX=nX: e.wait_ge(dX.h, nX))
        x_tile = lambda kc, c=c: cf(XB + kc * 2048, TC)
        xln_tile = lambda kc, c=c: cb(XLN + kc * 8192 + c * 1024, TC)
        _ln_chunk(ctx, kb, x_tile, xln_tile, cf(SCRB, TC),
                  dict(sq=sq, st=st, rw=rw, bc=bc, ap=ap_))
    ctx.pending.append((ap_, ap_.n))
    ctx.barrier()

    NPJ = 81
    for j in range(NPJ):
        wid = 128 if j < 80 else 64
        slot = j % 2

        def f_lw(e, j=j, slot=slot, wid=wid, gate=mm.n - 8):
            if gate > 0:
                e.wait_ge(mm.h, gate)
            e.dma_start(out=cb(WB + slot * 4096, 16 * 128)
                        .rearrange("p (a n) -> p a n", n=128)[:, :, 0:wid],
                        in_=kb.d_inw.rearrange("(a p) n -> p a n", p=128)
                        [:, :, j * 128:j * 128 + wid]).then_inc(dws[slot].h, 16)
        ctx.emit("gpsimd", f_lw)
        ndw = dws[slot].bump(16)

        for tc in range(NTC):
            pq = pool.acquire(ctx)

            def f_mm(e, j=j, tc=tc, slot=slot, pq=pq, ndw=ndw, wid=wid):
                if tc == 0:
                    e.wait_ge(dws[slot].h, ndw)
                for kbk in range(NKC):
                    m = e.matmul(out=pq.f32(wid, TC),
                                 lhsT=cb(WB + slot * 4096 + kbk * 256, 128)[:, 0:wid],
                                 rhs=cb(XLN + kbk * 8192 + tc * 1024, TC),
                                 start=(kbk == 0), stop=(kbk == NKC - 1))
                m.then_inc(mm.h, 1)
            ctx.emit("tensor", f_mm)
            nmm = mm.bump(1)

            def f_ev(e, j=j, tc=tc, pq=pq, nmm=nmm, wid=wid, sgate=so.n):
                e.wait_ge(mm.h, nmm)
                if tc == 0 and sgate > 0:
                    e.wait_ge(so.h, sgate)
                if wid == 128:
                    e.tensor_copy(out=cb(PS + tc * 1024, TC),
                                  in_=pq.f32(128, TC)).then_inc(pq.done.h, 1)
                else:
                    e.tensor_copy(out=cf(PSD + tc * 2048, TC, 64),
                                  in_=pq.f32(64, TC)).then_inc(pq.done.h, 1)
                e.drain().then_inc(evs.h, 1)
            ctx.emit("vector", f_ev)
            pq.done.bump(1)
            pool.release(pq)
            evs.bump(1)

        def f_st(e, j=j, nev=evs.n):
            e.wait_ge(evs.h, nev)
            if j < 32:
                e.dma_start(out=kb.d_projG[j * 128:(j + 1) * 128, :],
                            in_=cb(PS, 4096)).then_inc(so.h, 16)
            elif j < 80:
                e.dma_start(out=kb.d_projH[(j - 32) * 128:(j - 31) * 128, :],
                            in_=cb(PS, 4096)).then_inc(so.h, 16)
            else:
                e.dma_start(out=kb.d_projD,
                            in_=cf(PSD, 4096, 64)).then_inc(so.h, 16)
        ctx.emit("sync", f_st)
        so.bump(16)

    ctx.pending.append((so, so.n))
    ctx.flush()


def _phase_B2(ctx, kb):
    cf, cb, ci = kb.cf, kb.cb, kb.ci
    pool = kb.pool

    GT = 0
    HB = 16384
    DTR = 41344
    CV = 42368
    HT = 66944
    BT = 83328
    ACM = 87424
    DTF = 88448
    ABT = 89472
    ONE1 = 90496
    AdT = 91520
    DDT = 92544
    LSB = 93056
    AEB = 93312
    CBM = 93568
    EM = 97664
    MT = 99712
    YB = 103808
    SA = 120192
    HSD = 152960
    CEX = 155008
    DIA = 159104
    EXA = 159360
    SEL = 160384
    CVS = 176768
    SGT = CVS
    SQ = CVS + 2048

    dL = ctx.sem(); cvv = ctx.sem(); cva = ctx.sem(); dtv = ctx.sem()
    tr = ctx.sem(); rowv = ctx.sem(); gmm = ctx.sem(); hv = ctx.sem()
    yv = ctx.sem(); sv = ctx.sem(); gv = ctx.sem(); stt_ = ctx.sem()
    yn = ctx.sem(); ck = ctx.sem(); gsel = ctx.sem(); ms = ctx.sem()
    cky = ctx.sem()
    prev = {"gv": 0, "cva": 0, "dt1": 0, "sv": 0, "yv": 0, "stt": 0, "ck": 0,
            "tr": 0, "hv": 0, "rowv": 0, "gmm": 0, "cvv": 0}

    def f_prep(e):
        for hh in range(32):
            reg = cf(SEL + hh * 512, 128, 64)
            e.memset(reg, 1.0)
            for half in range(2):
                sub = reg[half * 32:half * 32 + 32, :]
                e.drain()
                e.affine_select(out=sub, in_=sub, compare_op=ALU.is_ge, fill=0.0,
                                base=-hh, pattern=[[0, 128]], channel_multiplier=1)
                e.drain()
                e.affine_select(out=sub, in_=sub, compare_op=ALU.is_ge, fill=0.0,
                                base=hh, pattern=[[0, 128]], channel_multiplier=-1)
        e.memset(cf(ONE1, 256, 64), 1.0)
        e.drain().then_inc(gsel.h, 1)
    ctx.emit("gpsimd", f_prep)
    gsel.bump(1)
    for eng in ("tensor", "vector", "scalar", "sync"):
        ctx.emit(eng, lambda e, n=gsel.n: e.wait_ge(gsel.h, n))

    for g in range(16):
        seq = g // 8
        ci_ = g % 8
        T0 = seq * SEQ + ci_ * CH
        sp = g % 2

        lo = max(T0 - 1, seq * SEQ)
        hi = min(T0 + CH + 2, (seq + 1) * SEQ)
        ofs = lo - (T0 - 1)

        def f_ld(e, T0=T0, lo=lo, hi=hi, ofs=ofs, g=g,
                 ggate=prev["gv"], cgate=prev["cva"], dgate=prev["dt1"],
                 rgate=prev["rowv"]):
            if ggate > 0:
                e.wait_ge(gv.h, ggate)
            if cgate > 0:
                e.wait_ge(cva.h, cgate)
            if dgate > 0:
                e.wait_ge(dtv.h, dgate)
            e.dma_start(out=cb(GT, 32 * 256).rearrange("p (a t) -> p a t", t=256),
                        in_=kb.d_projG.rearrange("(a p) t -> p a t", p=128)
                        [:, :, T0:T0 + CH]).then_inc(dL.h, 16)
            e.dma_start(out=cb(HB, 48 * 260).rearrange("p (a t) -> p a t", t=260)
                        [:, :, ofs:ofs + (hi - lo)],
                        in_=kb.d_projH.rearrange("(a p) t -> p a t", p=128)
                        [:, :, lo:hi]).then_inc(dL.h, 16)
            e.dma_start(out=cf(DTR, 256, 64),
                        in_=kb.d_projD[:, T0:T0 + CH]).then_inc(dL.h, 16)
        ctx.emit("sync", f_ld)
        nld = dL.bump(48)

        def f_mz(e, g=g, sp=sp, ofs=ofs, hi=hi, lo=lo, nld=nld, svg=prev["sv"]):
            e.wait_ge(dL.h, nld)
            if g % 8 == 0:
                if svg > 0:
                    e.wait_ge(sv.h, svg)
                e.memset(cf(SA + sp * 16384, 4096), 0.0)
            if ofs > 0:
                e.memset(cb(HB, 48 * 260).rearrange("p (a t) -> p a t", t=260)
                         [:, :, 0:ofs], 0.0)
            tail = (CH + 3) - (ofs + (hi - lo))
            if tail > 0:
                e.memset(cb(HB, 48 * 260).rearrange("p (a t) -> p a t", t=260)
                         [:, :, ofs + (hi - lo):CH + 3], 0.0)
            e.drain().then_inc(ms.h, 1)
        ctx.emit("gpsimd", f_mz)
        nms = ms.bump(1)

        for cb48 in range(48):
            csl = cb48 % 2

            def f_cv(e, cb48=cb48, csl=csl, nld=nld, nms=nms, agate=cva.n - 1,
                     yg=prev["yv"], sg_=prev["sv"], cg=prev["stt"], gg=prev["gmm"],
                     cvg=prev["cvv"]):
                if cb48 == 0:
                    e.wait_ge(dL.h, nld)
                    e.wait_ge(ms.h, nms)
                    if yg > 0:
                        e.wait_ge(yv.h, yg)
                    if sg_ > 0:
                        e.wait_ge(sv.h, sg_)
                    if cg > 0:
                        e.wait_ge(stt_.h, cg)
                    if gg > 0:
                        e.wait_ge(gmm.h, gg)
                    if cvg > 0:
                        e.wait_ge(cvv.h, cvg)
                if agate > 0:
                    e.wait_ge(cva.h, agate)
                A0 = CVS + csl * 2048
                A1 = A0 + 1024
                hbrow = lambda k: cb(HB + cb48 * 520 + 2 * k, 256)
                e.tensor_scalar(out=cf(A0, 256), in0=hbrow(0),
                                scalar1=kb.sconvw[:, cb48 * 4:cb48 * 4 + 1],
                                scalar2=None, op0=ALU.mult)
                e.drain()
                e.scalar_tensor_tensor(out=cf(A1, 256), in0=hbrow(1),
                                       scalar=kb.sconvw[:, cb48 * 4 + 1:cb48 * 4 + 2],
                                       in1=cf(A0, 256), op0=ALU.mult, op1=ALU.add)
                e.drain()
                e.scalar_tensor_tensor(out=cf(A0, 256), in0=hbrow(2),
                                       scalar=kb.sconvw[:, cb48 * 4 + 2:cb48 * 4 + 3],
                                       in1=cf(A1, 256), op0=ALU.mult, op1=ALU.add)
                e.drain()
                e.scalar_tensor_tensor(out=cf(A1, 256), in0=hbrow(3),
                                       scalar=kb.sconvw[:, cb48 * 4 + 3:cb48 * 4 + 4],
                                       in1=cf(A0, 256), op0=ALU.mult,
                                       op1=ALU.add).then_inc(cvv.h, 1)
            ctx.emit("vector", f_cv)
            ncv = cvv.bump(1)

            def f_si(e, cb48=cb48, csl=csl, ncv=ncv):
                e.wait_ge(cvv.h, ncv)
                e.activation(out=cb(CV + cb48 * 512, 256),
                             in_=cf(CVS + csl * 2048 + 1024, 256),
                             func=AF.Silu,
                             bias=kb.sconvb[:, cb48:cb48 + 1]).then_inc(cva.h, 1)
            ctx.emit("scalar", f_si)
            cva.bump(1)
        ncva = cva.n

        def f_dt1(e, nld=nld, gg=prev["gmm"], rg=prev["rowv"], tg=prev["tr"]):
            e.wait_ge(dL.h, nld)
            if gg > 0:
                e.wait_ge(gmm.h, gg)
            if rg > 0:
                e.wait_ge(rowv.h, rg)
            if tg > 0:
                e.wait_ge(tr.h, tg)
            e.tensor_scalar(out=cf(DTF, 256, 64), in0=cf(DTR, 256, 64),
                            scalar1=kb.sDtb[:], scalar2=None,
                            op0=ALU.add).then_inc(dtv.h, 1)
        ctx.emit("vector", f_dt1)
        n1 = dtv.bump(1)

        def f_dt2(e, n1=n1):
            e.wait_ge(dtv.h, n1)
            e.activation(out=cf(ABT, 256, 64), in_=cf(DTF, 256, 64), func=AF.Abs)
            e.drain()
            e.activation(out=cf(ABT, 256, 64), in_=cf(ABT, 256, 64), func=AF.Exp,
                         scale=-1.0)
            e.drain()
            e.activation(out=cf(ABT, 256, 64), in_=cf(ABT, 256, 64), func=AF.Ln,
                         bias=kb.sconst[0:64, 2:3]).then_inc(dtv.h, 1)
        ctx.emit("scalar", f_dt2)
        n2 = dtv.bump(1)

        def f_dt3(e, n2=n2):
            e.wait_ge(dtv.h, n2)
            e.scalar_tensor_tensor(out=cf(DTF, 256, 64), in0=cf(DTF, 256, 64),
                                   scalar=0.0, in1=cf(ABT, 256, 64),
                                   op0=ALU.max, op1=ALU.add)
            e.drain()
            e.tensor_scalar(out=cf(ABT, 256, 64), in0=cf(DTF, 256, 64),
                            scalar1=kb.sAcol[:], scalar2=None, op0=ALU.mult)
            e.drain()
            e.tensor_tensor_scan(out=cf(ACM, 256, 64), data0=cf(ONE1, 256, 64),
                                 data1=cf(ABT, 256, 64), initial=0.0,
                                 op0=ALU.mult, op1=ALU.add)
            e.drain().then_inc(dtv.h, 1)
        ctx.emit("vector", f_dt3)
        n3 = dtv.bump(1)

        def f_exa(e, n3=n3):
            e.wait_ge(dtv.h, n3)
            e.activation(out=cf(EXA, 256, 64), in_=cf(ACM, 256, 64),
                         func=AF.Exp).then_inc(dtv.h, 1)
        ctx.emit("scalar", f_exa)
        n4 = dtv.bump(1)

        for lt in range(2):
            for (srcoff, dsto) in ((ACM, 0), (DTF, 64)):
                pt = pool.acquire(ctx)

                def f_tp(e, pt=pt, lt=lt, srcoff=srcoff, n3=n4):
                    e.wait_ge(dtv.h, n3)
                    e.transpose(out=pt.f32(128, 64),
                                in_=cf(srcoff + lt * 512, 128, 64),
                                identity=kb.ident_f[0:64, 0:64]).then_inc(tr.h, 1)
                ctx.emit("tensor", f_tp)
                ntr = tr.bump(1)

                def f_te(e, pt=pt, lt=lt, dsto=dsto, ntr=ntr):
                    e.wait_ge(tr.h, ntr)
                    e.tensor_copy(out=cf(AdT + lt * 512 + dsto * 4, 64),
                                  in_=pt.f32(128, 64)).then_inc(pt.done.h, 1)
                    e.drain().then_inc(rowv.h, 1)
                ctx.emit("vector", f_te)
                pt.done.bump(1)
                pool.release(pt)
                rowv.bump(1)

        pa = pool.acquire(ctx)

        def f_dia(e, n4=n4):
            e.wait_ge(dtv.h, n4)
            e.tensor_scalar(out=cf(DIA, 64, 64), in0=kb.ident_f[0:64, 0:64],
                            scalar1=cf(ACM + 255 * 4, 1, 64), scalar2=None,
                            op0=ALU.mult)
            e.drain().then_inc(rowv.h, 1)
        ctx.emit("vector", f_dia)
        n5 = rowv.bump(1)

        def f_ab(e, pa=pa, n5=n5):
            e.wait_ge(rowv.h, n5)
            e.matmul(out=pa.f32(128, 64), lhsT=kb.ones_f[0:64, :],
                     rhs=cf(DIA, 64, 64), start=True, stop=True).then_inc(tr.h, 1)
        ctx.emit("tensor", f_ab)
        ntr2 = tr.bump(1)

        def f_lam(e, pa=pa, ntr2=ntr2):
            e.wait_ge(tr.h, ntr2)
            e.activation(out=cf(LSB, 64), in_=pa.f32(128, 64),
                         func=AF.Exp).then_inc(rowv.h, 1)
        ctx.emit("scalar", f_lam)
        n6 = rowv.bump(1)

        def f_aeb(e, pa=pa, n6=n6):
            e.wait_ge(rowv.h, n6)
            e.tensor_copy(out=cf(AEB, 64), in_=pa.f32(128, 64)).then_inc(pa.done.h, 1)
            e.drain().then_inc(rowv.h, 1)
        ctx.emit("vector", f_aeb)
        pa.done.bump(1)
        pool.release(pa)
        n7 = rowv.bump(1)

        for lt in range(2):
            def f_ddt1(e, lt=lt, n7=n7, gate=rowv.n):
                if lt == 0:
                    e.wait_ge(rowv.h, n7)
                else:
                    e.wait_ge(rowv.h, gate)
                e.tensor_tensor(out=cf(EM, 64), in0=cf(AEB, 64),
                                in1=cf(AdT + lt * 512, 64),
                                op=ALU.subtract).then_inc(rowv.h, 1)
            ctx.emit("vector", f_ddt1)
            n8 = rowv.bump(1)

            def f_ddt2(e, lt=lt, n8=n8):
                e.wait_ge(rowv.h, n8)
                e.activation(out=cf(EM, 64), in_=cf(EM, 64),
                             func=AF.Exp).then_inc(rowv.h, 1)
            ctx.emit("scalar", f_ddt2)
            n9 = rowv.bump(1)

            def f_ddt3(e, lt=lt, n9=n9):
                e.wait_ge(rowv.h, n9)
                e.tensor_tensor(out=cf(DDT + lt * 256, 64), in0=cf(EM, 64),
                                in1=cf(AdT + lt * 512 + 256, 64),
                                op=ALU.mult).then_inc(rowv.h, 1)
            ctx.emit("vector", f_ddt3)
            rowv.bump(1)
        nrows = rowv.n

        for pr in range(16):
            for lt in range(2):
                pt = pool.acquire(ctx)

                def f_tp(e, pt=pt, pr=pr, lt=lt, ncva=ncva):
                    if pr == 0 and lt == 0:
                        e.wait_ge(cva.h, ncva)
                    e.transpose(out=pt.bf16(128, 128),
                                in_=cb(CV + pr * 2 * 512 + lt * 256, 128),
                                identity=kb.ident_bf[:]).then_inc(tr.h, 1)
                ctx.emit("tensor", f_tp)
                ntr = tr.bump(1)

                def f_te(e, pt=pt, pr=pr, lt=lt, ntr=ntr):
                    e.wait_ge(tr.h, ntr)
                    e.tensor_copy(out=cb(HT + lt * 8192 + pr * 256, 128),
                                  in_=pt.bf16(128, 128)).then_inc(pt.done.h, 1)
                    e.drain().then_inc(hv.h, 1)
                ctx.emit("vector", f_te)
                pt.done.bump(1)
                pool.release(pt)
                hv.bump(1)
        for g8 in range(8):
            for lt in range(2):
                pt = pool.acquire(ctx)

                def f_tp(e, pt=pt, g8=g8, lt=lt, ncva=ncva):
                    e.transpose(out=pt.bf16(128, 128),
                                in_=cb(CV + (32 + g8) * 512 + lt * 256, 128),
                                identity=kb.ident_bf[:]).then_inc(tr.h, 1)
                ctx.emit("tensor", f_tp)
                ntr = tr.bump(1)

                def f_te(e, pt=pt, g8=g8, lt=lt, ntr=ntr):
                    e.wait_ge(tr.h, ntr)
                    e.tensor_copy(out=cb(BT + (lt * 8 + g8) * 256, 128),
                                  in_=pt.bf16(128, 128)).then_inc(pt.done.h, 1)
                    e.drain().then_inc(hv.h, 1)
                ctx.emit("vector", f_te)
                pt.done.bump(1)
                pool.release(pt)
                hv.bump(1)
        nhv = hv.n

        for g8 in range(8):
            gs = g8 % 2
            for st_ in range(2):
                pc = pool.acquire(ctx)

                def f_cb(e, pc=pc, g8=g8, st_=st_, ncva=ncva):
                    if g8 == 0 and st_ == 0:
                        e.wait_ge(cva.h, ncva)
                    e.matmul(out=pc.f32(128, 256),
                             lhsT=cb(CV + (32 + g8) * 512 + st_ * 256, 128),
                             rhs=cb(CV + (40 + g8) * 512, 256),
                             start=True, stop=True).then_inc(gmm.h, 1)
                ctx.emit("tensor", f_cb)
                ngm = gmm.bump(1)

                def f_ms(e, pc=pc, g8=g8, gs=gs, st_=st_, ngm=ngm,
                         mgate=stt_.n - 2 * 16):
                    e.wait_ge(gmm.h, ngm)
                    if mgate > 0:
                        e.wait_ge(stt_.h, mgate)
                    e.tensor_tensor(out=cf(CBM + (gs * 2 + st_) * 1024, 256),
                                    in0=pc.f32(128, 256),
                                    in1=kb.tri[:, st_ * 256:(st_ + 1) * 256],
                                    op=ALU.mult).then_inc(pc.done.h, 1)
                    e.drain().then_inc(cvv.h, 1)
                ctx.emit("vector", f_ms)
                pc.done.bump(1)
                pool.release(pc)
                cvv.bump(1)
            ncbm = cvv.n

            for hh8 in range(8):
                h = g8 * 8 + hh8
                hrot = h % 4
                pr_a = pool.acquire(ctx)
                pr_e = pool.acquire(ctx)

                def f_sel(e, h=h, pr_a=pr_a, pr_e=pr_e, n4=n4, nrows=nrows,
                          first=(g8 == 0 and hh8 == 0), svg=prev["sv"]):
                    e.wait_ge(dtv.h, n4)
                    e.wait_ge(rowv.h, nrows)
                    if first and svg > 0:
                        e.wait_ge(sv.h, svg)
                    half = h // 32
                    sel = cf(SEL + (h % 32) * 512, 128, 64)[half * 32:half * 32 + 32, :]
                    e.matmul(out=pr_a.f32(128, 256), lhsT=sel,
                             rhs=cf(ACM, 256, 64)[half * 32:half * 32 + 32, :],
                             start=True, stop=True).then_inc(gmm.h, 1)
                    e.matmul(out=pr_e.f32(128, 256), lhsT=sel,
                             rhs=cf(EXA, 256, 64)[half * 32:half * 32 + 32, :],
                             start=True, stop=True).then_inc(gmm.h, 1)
                ctx.emit("tensor", f_sel)
                nga = gmm.bump(1)
                nge = gmm.bump(1)

                for st_ in range(2):
                    def f_e1(e, pr_a=pr_a, h=h, st_=st_, nga=nga, hrot=hrot,
                             mgate=gv.n - 2 * 2):
                        if st_ == 0:
                            e.wait_ge(gmm.h, nga)
                        if mgate > 0:
                            e.wait_ge(gv.h, mgate)
                        e.tensor_scalar(out=cf(EM + st_ * 1024, 256),
                                        in0=pr_a.f32(128, 256),
                                        scalar1=cf(AdT + st_ * 512 + h * 4, 1, 128)
                                        if False else cf(AdT + st_ * 512 + h * 4, 1),
                                        scalar2=0.0, op0=ALU.subtract,
                                        op1=ALU.min).then_inc(stt_.h, 1)
                    ctx.emit("vector", f_e1)
                    ne1 = stt_.bump(1)

                    def f_e2(e, st_=st_, ne1=ne1):
                        e.wait_ge(stt_.h, ne1)
                        e.activation(out=cf(EM + st_ * 1024, 256),
                                     in_=cf(EM + st_ * 1024, 256),
                                     func=AF.Exp).then_inc(stt_.h, 1)
                    ctx.emit("scalar", f_e2)
                    ne2 = stt_.bump(1)

                    def f_m(e, st_=st_, gs=gs, h=h, hrot=hrot, ne2=ne2):
                        e.wait_ge(stt_.h, ne2)
                        e.scalar_tensor_tensor(out=cb(MT + (hrot * 2 + st_) * 512, 256),
                                               in0=cf(CBM + (gs * 2 + st_) * 1024, 256),
                                               scalar=cf(AdT + st_ * 512 + 256 + h * 4, 1),
                                               in1=cf(EM + st_ * 1024, 256),
                                               op0=ALU.mult,
                                               op1=ALU.mult).then_inc(stt_.h, 1)
                    ctx.emit("vector", f_m)
                    stt_.bump(1)
                nmt = stt_.n

                def f_cx(e, g8=g8, hrot=hrot, pr_e=pr_e, nge=nge,
                         cgate=sv.n - 4):
                    e.wait_ge(gmm.h, nge)
                    if cgate > 0:
                        e.wait_ge(sv.h, cgate)
                    e.tensor_tensor(out=cf(CEX + hrot * 1024, 256),
                                    in0=cb(CV + (40 + g8) * 512, 256),
                                    in1=pr_e.f32(128, 256),
                                    op=ALU.mult).then_inc(pr_e.done.h, 1)
                    e.drain().then_inc(stt_.h, 1)
                ctx.emit("vector", f_cx)
                pr_e.done.bump(1)
                pool.release(pr_e)
                ncx = stt_.bump(1)

                py = pool.acquire(ctx)
                hbase = 64 * (h % 2)

                def f_y(e, h=h, hrot=hrot, py=py, sp=sp, nmt=nmt, ncx=ncx,
                        hbase=hbase, pr_a=pr_a):
                    e.wait_ge(stt_.h, max(nmt, ncx))
                    for st_ in range(2):
                        e.matmul(out=py.f32(128, 256)[hbase:hbase + 64, :],
                                 lhsT=cb(HT + st_ * 8192 + h * 128, 64),
                                 rhs=cb(MT + (hrot * 2 + st_) * 512, 256),
                                 start=(st_ == 0), stop=False)
                    m = e.matmul(out=py.f32(128, 256)[hbase:hbase + 64, :],
                                 lhsT=cf(SA + sp * 16384 + h * 256, 64),
                                 rhs=cf(CEX + hrot * 1024, 256),
                                 start=False, stop=True)
                    m.then_inc(gmm.h, 1)
                    e.nop().then_inc(pr_a.done.h, 1)
                ctx.emit("tensor", f_y)
                pr_a.done.bump(1)
                pool.release(pr_a)
                ny = gmm.bump(1)

                def f_ye(e, h=h, py=py, ny=ny, hbase=hbase,
                         first=(h == 0), ckg=prev["ck"]):
                    e.wait_ge(gmm.h, ny)
                    if first and ckg > 0:
                        e.wait_ge(cky.h, ckg)
                    e.scalar_tensor_tensor(
                        out=cb(YB + (h // 2) * 512, 256)[hbase:hbase + 64, :],
                        in0=cb(CV + (h // 2) * 512, 256)[hbase:hbase + 64, :],
                        scalar=kb.sDbc[hbase:hbase + 64, h:h + 1],
                        in1=py.f32(128, 256)[hbase:hbase + 64, :],
                        op0=ALU.mult, op1=ALU.add).then_inc(py.done.h, 1)
                    e.drain().then_inc(yv.h, 1)
                ctx.emit("vector", f_ye)
                py.done.bump(1)
                pool.release(py)
                yv.bump(1)

                for lt in range(2):
                    def f_hd(e, h=h, lt=lt, hrot=hrot, nhv=nhv, nrows=nrows,
                             hgate=sv.n - 4):
                        if lt == 0:
                            e.wait_ge(hv.h, nhv)
                            e.wait_ge(rowv.h, nrows)
                            if hgate > 0:
                                e.wait_ge(sv.h, hgate)
                        e.tensor_scalar(out=cb(HSD + (hrot * 2 + lt) * 256, 128)
                                        [:, 0:64],
                                        in0=cb(HT + lt * 8192 + h * 128, 64),
                                        scalar1=cf(DDT + lt * 256 + h * 4, 1),
                                        scalar2=None,
                                        op0=ALU.mult).then_inc(stt_.h, 1)
                    ctx.emit("vector", f_hd)
                    stt_.bump(1)
                nhsd = stt_.n
                ps_ = pool.acquire(ctx)

                def f_sc(e, h=h, g8=g8, hrot=hrot, ps_=ps_, nhsd=nhsd):
                    e.wait_ge(stt_.h, nhsd)
                    for lt in range(2):
                        m = e.matmul(out=ps_.f32(128, 64),
                                     lhsT=cb(BT + (lt * 8 + g8) * 256, 128),
                                     rhs=cb(HSD + (hrot * 2 + lt) * 256, 128)[:, 0:64],
                                     start=(lt == 0), stop=(lt == 1))
                    m.then_inc(gmm.h, 1)
                ctx.emit("tensor", f_sc)
                nsc = gmm.bump(1)

                def f_su(e, h=h, sp=sp, ps_=ps_, nsc=nsc):
                    e.wait_ge(gmm.h, nsc)
                    e.scalar_tensor_tensor(out=cf(SA + (1 - sp) * 16384 + h * 256, 64),
                                           in0=cf(SA + sp * 16384 + h * 256, 64),
                                           scalar=cf(LSB + h * 4, 1),
                                           in1=ps_.f32(128, 64),
                                           op0=ALU.mult, op1=ALU.add).then_inc(ps_.done.h, 1)
                    e.drain().then_inc(sv.h, 1)
                ctx.emit("vector", f_su)
                ps_.done.bump(1)
                pool.release(ps_)
                sv.bump(1)
        nyall = yv.n
        nsall = sv.n

        pq1 = pool.acquire(ctx)
        pq2 = pool.acquire(ctx)
        for ib in range(32):
            sl = ib % 2

            def f_sg(e, ib=ib, sl=sl, nld=nld, nyall=nyall, ygate=gv.n - 2):
                if ib == 0:
                    e.wait_ge(dL.h, nld)
                if ygate > 0:
                    e.wait_ge(gv.h, ygate)
                e.activation(out=cf(SGT + sl * 1024, 256),
                             in_=cb(GT + ib * 512, 256),
                             func=AF.Silu).then_inc(gv.h, 1)
            ctx.emit("scalar", f_sg)
            nsg = gv.bump(1)

            def f_yf(e, ib=ib, sl=sl, nsg=nsg, nyall=nyall):
                e.wait_ge(gv.h, nsg)
                if ib == 0:
                    e.wait_ge(yv.h, nyall)
                e.tensor_tensor(out=cb(YB + ib * 512, 256),
                                in0=cb(YB + ib * 512, 256),
                                in1=cf(SGT + sl * 1024, 256),
                                op=ALU.mult).then_inc(gv.h, 1)
            ctx.emit("vector", f_yf)
            nyf = gv.bump(1)

            def f_sq(e, ib=ib, sl=sl, nyf=nyf, sgate=gmm.n - 1):
                e.wait_ge(gv.h, nyf)
                if sgate > 0:
                    e.wait_ge(gmm.h, sgate)
                e.activation(out=cf(SQ + sl * 1024, 256),
                             in_=cb(YB + ib * 512, 256),
                             func=AF.Square).then_inc(gv.h, 1)
            ctx.emit("scalar", f_sq)
            nsq2 = gv.bump(1)

            def f_st(e, ib=ib, sl=sl, pq1=pq1, nsq2=nsq2):
                e.wait_ge(gv.h, nsq2)
                e.matmul(out=pq1.f32(1, 256), lhsT=kb.ones_f[:, 0:1],
                         rhs=cf(SQ + sl * 1024, 256), start=(ib == 0),
                         stop=(ib == 31)).then_inc(gmm.h, 1)
            ctx.emit("tensor", f_st)
            gmm.bump(1)
        nstats = gmm.n

        def f_rs(e, pq1=pq1, nstats=nstats):
            e.wait_ge(gmm.h, nstats)
            e.tensor_scalar(out=kb.srow[0:1, 0:256], in0=pq1.f32(1, 256),
                            scalar1=1.0 / INTER, scalar2=None,
                            op0=ALU.mult).then_inc(pq1.done.h, 1)
            e.drain().then_inc(yn.h, 1)
        ctx.emit("vector", f_rs)
        pq1.done.bump(1)
        pool.release(pq1)
        nr1 = yn.bump(1)

        def f_rs2(e, nr1=nr1):
            e.wait_ge(yn.h, nr1)
            e.activation(out=kb.srow[0:1, 0:256], in_=kb.srow[0:1, 0:256],
                         func=AF.Sqrt, bias=kb.sconst[0:1, 1:2]).then_inc(yn.h, 1)
        ctx.emit("scalar", f_rs2)
        nr2 = yn.bump(1)

        def f_rs3(e, nr2=nr2):
            e.wait_ge(yn.h, nr2)
            e.reciprocal(out=kb.srow[0:1, 0:256],
                         in_=kb.srow[0:1, 0:256]).then_inc(yn.h, 1)
        ctx.emit("vector", f_rs3)
        nr3 = yn.bump(1)

        def f_rb(e, pq2=pq2, nr3=nr3):
            e.wait_ge(yn.h, nr3)
            e.matmul(out=pq2.f32(128, 256), lhsT=kb.ones_f[0:1, :],
                     rhs=kb.srow[0:1, 0:256], start=True, stop=True).then_inc(gmm.h, 1)
        ctx.emit("tensor", f_rb)
        nrb = gmm.bump(1)

        for ib in range(32):
            def f_yn(e, ib=ib, pq2=pq2, nrb=nrb):
                if ib == 0:
                    e.wait_ge(gmm.h, nrb)
                m = e.scalar_tensor_tensor(out=cb(YB + ib * 512, 256),
                                           in0=cb(YB + ib * 512, 256),
                                           scalar=kb.snormw[:, ib:ib + 1],
                                           in1=pq2.f32(128, 256),
                                           op0=ALU.mult, op1=ALU.mult)
                if ib == 31:
                    m.then_inc(pq2.done.h, 1)
                e.drain().then_inc(yn.h, 1)
            ctx.emit("vector", f_yn)
            yn.bump(1)
        pq2.done.bump(1)
        pool.release(pq2)
        nynd = yn.n

        def f_sy(e, T0=T0, nynd=nynd):
            e.wait_ge(yn.h, nynd)
            e.dma_start(out=kb.d_ynT.rearrange("(a p) t -> p a t", p=128)
                        [:, :, T0:T0 + CH],
                        in_=cb(YB, 32 * 256).rearrange("p (a t) -> p a t", t=256)
                        ).then_inc(cky.h, 16)
        ctx.emit("sync", f_sy)
        nck1 = cky.bump(16)

        def f_ckv(e, nsall=nsall):
            e.wait_ge(sv.h, nsall)
            e.drain().then_inc(ck.h, 1)
        ctx.emit("vector", f_ckv)
        ck.bump(1)
        prev = {"gv": gv.n, "cva": cva.n, "dt1": dtv.n, "sv": sv.n,
                "yv": yv.n, "stt": stt_.n, "ck": nck1, "tr": tr.n,
                "hv": hv.n, "rowv": rowv.n, "gmm": gmm.n, "cvv": cvv.n}

    ctx.pending.append((ck, ck.n))
    ctx.pending.append((cky, cky.n))
    ctx.flush()


def _phase_B3(ctx, kb):
    cf, cb = kb.cf, kb.cb
    pool = kb.pool

    OUTW = 0
    YS = 131072
    X2S = 163840
    X3S = 167936

    dW = ctx.sem(); dY = ctx.sem(); mm = ctx.sem(); ev = ctx.sem()
    dx2s = [ctx.sem(), ctx.sem()]
    sos = [ctx.sem(), ctx.sem()]

    def f_lw(e):
        for ib in range(32):
            e.dma_start(out=cb(OUTW + ib * 4096, 2048),
                        in_=kb.d_outw[ib * 128:(ib + 1) * 128, :]).then_inc(dW.h, 16)
    ctx.emit("gpsimd", f_lw)
    nW = dW.bump(16 * 32)

    for tc in range(NTC):
        def f_ly(e, tc=tc, gate=mm.n):
            if gate > 0:
                e.wait_ge(mm.h, gate)
            e.dma_start(out=sb3(cb(YS, 32 * TC)),
                        in_=kb.d_ynT.rearrange("k (c t) -> k c t", t=TC)[:, tc, :]
                        .rearrange("(a p) t -> p a t", p=128)).then_inc(dY.h, 16)
        ctx.emit("sync", f_ly)
        ndy = dY.bump(16)

        for j in range(NKC):
            slot = j % 2
            pq = pool.acquire(ctx)

            def f_lx(e, j=j, tc=tc, slot=slot, egate=ev.n - 1):
                if egate > 0:
                    e.wait_ge(ev.h, egate)
                e.dma_start(out=cf(X2S + slot * 2048, 512),
                            in_=kb.d_x2.rearrange("k (c t) -> k c t", t=TC)
                            [j * 128:(j + 1) * 128, tc, :]).then_inc(dx2s[slot].h, 16)
            ctx.emit("sync", f_lx)
            ndx = dx2s[slot].bump(16)

            def f_mm(e, j=j, tc=tc, pq=pq, ndy=ndy, nW=nW, first=(j == 0 and tc == 0)):
                if j == 0:
                    e.wait_ge(dY.h, ndy)
                if first:
                    e.wait_ge(dW.h, nW)
                for ib in range(32):
                    m = e.matmul(out=pq.f32(128, TC),
                                 lhsT=cb(OUTW + ib * 4096 + j * 256, 128),
                                 rhs=cb(YS + ib * 1024, TC),
                                 start=(ib == 0), stop=(ib == 31))
                m.then_inc(mm.h, 1)
            ctx.emit("tensor", f_mm)
            nmm = mm.bump(1)

            def f_ev(e, j=j, slot=slot, pq=pq, nmm=nmm, ndx=ndx,
                     sgate=sos[slot].n):
                e.wait_ge(mm.h, nmm)
                e.wait_ge(dx2s[slot].h, ndx)
                if sgate > 0:
                    e.wait_ge(sos[slot].h, sgate)
                e.tensor_tensor(out=cf(X3S + slot * 2048, 512),
                                in0=cf(X2S + slot * 2048, 512),
                                in1=pq.f32(128, TC), op=ALU.add).then_inc(pq.done.h, 1)
                e.drain().then_inc(ev.h, 1)
            ctx.emit("vector", f_ev)
            pq.done.bump(1)
            pool.release(pq)
            nev = ev.bump(1)

            def f_sx(e, j=j, tc=tc, slot=slot, nev=nev):
                e.wait_ge(ev.h, nev)
                e.dma_start(out=kb.d_x3.rearrange("k (c t) -> k c t", t=TC)
                            [j * 128:(j + 1) * 128, tc, :],
                            in_=cf(X3S + slot * 2048, 512)).then_inc(sos[slot].h, 16)
            ctx.emit("sync", f_sx)
            sos[slot].bump(16)

    ctx.pending.append((sos[0], sos[0].n))
    ctx.pending.append((sos[1], sos[1].n))
    ctx.flush()


def _host_maps(inputs):
    sig = _parity_perm(HID)
    tau = _parity_perm(MLP)
    f32 = np.float32

    Wq = np.asarray(inputs["L0_attn_a_Wq"], f32)
    Wo = np.asarray(inputs["L0_attn_a_Wo"], f32)
    mem = np.asarray(inputs["L0_attn_a_memories"], f32)
    mn = np.asarray(inputs["L0_attn_a_memory_norms"], f32)
    base = {
        "wqT": np.ascontiguousarray(Wq.T[sig]),
        "woT": np.ascontiguousarray(Wo.T[:, sig]),
        "memT": np.ascontiguousarray(mem.transpose(2, 0, 1, 3).reshape(128, -1)),
        "mnT": np.ascontiguousarray(mn.transpose(2, 1, 0).reshape(128, -1)),
        "bankw": np.ascontiguousarray(np.asarray(inputs["L0_attn_a_bank_weights"], f32)),
        "mnrows": np.ascontiguousarray(mn.reshape(NBANK, -1)),
        "inw": np.ascontiguousarray(np.asarray(inputs["L0_attn_b_in_proj_w"], f32)[sig]),
        "outw": np.ascontiguousarray(np.asarray(inputs["L0_attn_b_out_proj_w"], f32)[:, sig]),
        "convw": np.ascontiguousarray(
            np.asarray(inputs["L0_attn_b_conv_w"], f32).reshape(4, CONVD).T
            .reshape(48, 128, 4).transpose(1, 0, 2).reshape(128, 192)),
        "convb": np.ascontiguousarray(
            np.asarray(inputs["L0_attn_b_conv_b"], f32).reshape(48, 128).T),
        "dtb": np.ascontiguousarray(np.asarray(inputs["L0_attn_b_dt_bias"], f32).reshape(NH, 1)),
        "alog": np.ascontiguousarray(np.asarray(inputs["L0_attn_b_A_log"], f32).reshape(NH, 1)),
        "dvec": np.ascontiguousarray(np.asarray(inputs["L0_attn_b_D"], f32).reshape(NH, 1)),
        "normw": np.ascontiguousarray(
            np.asarray(inputs["L0_attn_b_norm_w"], f32).reshape(32, 128).T),
        "gpT": np.ascontiguousarray(np.asarray(inputs["L0_ffn_gate_W_packed"], np.int32)[tau].T),
        "gsT": np.ascontiguousarray(np.asarray(inputs["L0_ffn_gate_scales"], f32)[tau].T),
        "upT": np.ascontiguousarray(np.asarray(inputs["L0_ffn_up_W_packed"], np.int32)[tau].T),
        "usT": np.ascontiguousarray(np.asarray(inputs["L0_ffn_up_scales"], f32)[tau].T),
        "dpT": np.ascontiguousarray(np.asarray(inputs["L0_ffn_down_W_packed"], np.int32)[sig].T),
        "dsT": np.ascontiguousarray(np.asarray(inputs["L0_ffn_down_scales"], f32)[sig].T),
    }
    x = np.asarray(inputs["L0_attn_a_hidden_states"], f32)
    maps = []
    for c in range(8):
        m = dict(base)
        m["xt"] = np.ascontiguousarray(
            x[2 * c:2 * c + 2].transpose(2, 0, 1).reshape(HID, TCORE)[sig])
        maps.append(m)
    return maps, sig


def kernel(**inputs):
    maps, sig = _host_maps(inputs)
    nc = build(upto=8)
    res = run_bass_kernel_spmd(nc, maps, core_ids=list(range(8)))
    out = np.empty((16, SEQ, HID), np.float32)
    for c in range(8):
        t = np.empty((HID, TCORE), np.float32)
        t[sig] = res.results[c]["oxt"]
        out[2 * c:2 * c + 2] = t.reshape(HID, 2, SEQ).transpose(1, 2, 0)
    return out
